# version 05
# baseline (speedup 1.0000x reference)
import math
import contextlib
import numpy as np
import ml_dtypes
import concourse.bass as bass
import concourse.mybir as mybir
from concourse.bass_utils import run_bass_kernel_spmd

F32 = mybir.dt.float32
BF16 = mybir.dt.bfloat16
AF = mybir.ActivationFunctionType
ALU = mybir.AluOpType
AX = mybir.AxisListType
ds = bass.ds

NC = 8
B = 2
C = 4096
D = 2048
T = B * C
TSH = T // NC
NH = 16
HD = 128
NE = 16
EI = 1408
SHD = 2816
NSC = SHD // 128
NKC = D // 128
NIC = EI // 128
EPS = 1e-5
CQK = float(128.0 ** -0.25)
ISQH = float(1.0 / math.sqrt(128.0))
SQ128 = float(math.sqrt(128.0))
PERM = [2 * i for i in range(64)] + [2 * i + 1 for i in range(64)]

MODE = "fused"
_DEBUG_IO = False
SPARSE_MOE = True
NCAP = 3072
NCH = NCAP // 128
_CAPTURE = False
_STOP_AFTER = None
_DBG = {}



class SemCtr:
    __slots__ = ("h", "n")

    def __init__(self, h):
        self.h = h
        self.n = 0

    def tok(self):
        return (self, self.n)


class K:

    def __init__(self, nc):
        self.nc = nc
        self.stack = contextlib.ExitStack()

    def sem(self, name):
        h = self.stack.enter_context(self.nc.semaphore(name))
        return SemCtr(h)

    def sb(self, name, shape, dtype, scope=None):
        if scope is None:
            return self.nc.alloc_sbuf_tensor(name, list(shape), dtype)
        return scope.enter_context(self.nc.sbuf_tensor(name, list(shape), dtype))

    def ps(self, name, shape, dtype, scope=None):
        if scope is None:
            return self.stack.enter_context(self.nc.psum_tensor(name, list(shape), dtype))
        return scope.enter_context(self.nc.psum_tensor(name, list(shape), dtype))


class Rec:
    ENGS = ("sync", "gpsimd", "tensor", "vector", "scalar")

    def __init__(self):
        self.q = {e: [] for e in self.ENGS}
        self.wmax = {e: {} for e in self.ENGS}

    def emit(self, eng, fn, waits=(), inc=None, inck=1):
        wl = []
        wm = self.wmax[eng]
        for (s, v) in waits:
            if v > 0 and wm.get(id(s), -1) < v:
                wl.append((s.h, v))
                wm[id(s)] = v
        tok = None
        if inc is not None:
            inc.n += inck
            tok = (inc, inc.n)
            incp = (inc.h, inck)
        else:
            incp = None

        def closure(e, fn=fn, wl=wl, incp=incp):
            for h, v in wl:
                e.wait_ge(h, v)
            instr = fn(e)
            if incp is not None:
                instr.then_inc(*incp)

        self.q[eng].append(closure)
        return tok

    def flush(self, block):
        for name in self.ENGS:
            lst = self.q[name]
            if lst:
                getattr(block, name)(lambda e, lst=lst: [c(e) for c in lst])
            self.q[name] = []


def _rearr_kr(ap, r=128):
    return ap.rearrange("(k r) j -> r k j", r=r)



def phase1(kk, R, S, io, sb, sco=None):
    nc = kk.nc
    x_sh, gwT, xnT_dst, sm_dst, cmbT_dst = (
        io["x_sh"], io["gwT"], io["xnT_dst"], io["sm_dst"], io["cmbT_dst"])
    identb, identf = sb["identb"], sb["identf"]

    xt = [kk.sb("p1_xt0", [128, D], F32, sco), kk.sb("p1_xt1", [128, D], F32, sco)]
    sq = kk.sb("p1_sq", [128, D], F32, sco)
    xnb = [kk.sb("p1_xnb0", [128, D], BF16, sco), kk.sb("p1_xnb1", [128, D], BF16, sco)]
    xnT_my = kk.sb("p1_xnT", [128, NKC, TSH], BF16, sco)
    smtile = [kk.sb("p1_smt0", [128, 4], F32, sco), kk.sb("p1_smt1", [128, 4], F32, sco)]
    gw_sb = kk.sb("p1_gw", [128, NKC, 16], BF16, sco)
    st = {n: kk.sb("p1_" + n, [128, 1], F32, sco) for n in
          ("su", "ssu", "mu", "e2", "mu2", "var", "sd", "r", "mur", "rr",
           "u", "sd2", "s", "sc", "m", "m2t")}
    sg = kk.sb("p1_sg", [128, 16], F32, sco)
    gm = kk.sb("p1_gm", [128, 4], F32, sco)
    gm2 = kk.sb("p1_gm2", [128, 4], F32, sco)
    eq = kk.sb("p1_eq", [128, 4], F32, sco)
    emask = kk.sb("p1_em", [128, 16], F32, sco)
    ms0 = kk.sb("p1_ms0", [128, 16], F32, sco)
    msx = kk.sb("p1_msx", [128, 16], F32, sco)
    sel = kk.sb("p1_sel", [128, 16], F32, sco)
    wv_ = kk.sb("p1_wv", [128, 16], F32, sco)
    m1 = kk.sb("p1_m1", [128, 1], F32, sco)
    wsum = kk.sb("p1_ws", [128, 1], F32, sco)
    winv = kk.sb("p1_wi", [128, 1], F32, sco)
    cmb = kk.sb("p1_cmb", [128, 16], F32, sco)
    cmbT_st = kk.sb("p1_cmbT", [16, TSH], F32, sco)

    ps_xt = kk.ps("p1_psx", [128, 2048], BF16, sco)
    ps_g = kk.ps("p1_psg", [128, 16], F32, sco)
    ps_ct = kk.ps("p1_psct", [16, 128], F32, sco)

    t_gw = R.emit("gpsimd", lambda e: e.dma_start(
        out=gw_sb[:, :, :], in_=_rearr_kr(gwT.ap())), inc=S["dg"], inck=16)

    xt_free = [(), ()]
    xnb_free = [(), ()]
    smt_free = [(), ()]
    ssu_free = ()
    ps_g_free = ()
    ps_ct_free = ()
    xn_toks = []
    gate_toks = []
    smst_toks = []

    for t in range(TSH // 128):
        par = t % 2
        tl = R.emit("sync", lambda e, t=t, par=par: e.dma_start(
            out=xt[par][:, :], in_=x_sh[t * 128:(t + 1) * 128, :]),
            waits=xt_free[par], inc=S["dsy"], inck=16)
        tsq = R.emit("scalar", lambda e, par=par: e.activation(
            sq[:, :], xt[par][:, :], AF.Square, accum_out=st["ssu"][:, :]),
            waits=(tl,) + ssu_free, inc=S["s"])
        tsu = R.emit("vector", lambda e, par=par: e.reduce_sum(
            st["su"][:, :], xt[par][:, :], axis=AX.X), waits=(tl,), inc=S["v"])
        tmu = R.emit("vector", lambda e: e.tensor_scalar_mul(
            st["mu"][:, :], st["su"][:, :], 1.0 / D), waits=(tsu,), inc=S["v"])
        te2 = R.emit("vector", lambda e: e.tensor_scalar_mul(
            st["e2"][:, :], st["ssu"][:, :], 1.0 / D), waits=(tsq,), inc=S["v"])
        ssu_free = (te2,)
        tmu2 = R.emit("vector", lambda e: e.tensor_mul(
            st["mu2"][:, :], st["mu"][:, :], st["mu"][:, :]), waits=(tmu,), inc=S["v"])
        tvar = R.emit("vector", lambda e: e.tensor_sub(st["var"][:, :], st["e2"][:, :], st["mu2"][:, :]),
                      waits=(tmu2,), inc=S["v"])
        tsd = R.emit("scalar", lambda e: e.activation(
            st["sd"][:, :], st["var"][:, :], AF.Sqrt, bias=sb["epscol"][:, :]),
            waits=(tvar,), inc=S["s"])
        tr = R.emit("vector", lambda e: e.reciprocal(st["r"][:, :], st["sd"][:, :]),
                    waits=(tsd,), inc=S["v"])
        tmur = R.emit("vector", lambda e: e.tensor_mul(
            st["mur"][:, :], st["mu"][:, :], st["r"][:, :]), waits=(tr,), inc=S["v"])
        txn = R.emit("vector", lambda e, par=par: e.tensor_scalar(
            xnb[par][:, :], xt[par][:, :], st["r"][:, :], st["mur"][:, :],
            ALU.mult, ALU.subtract), waits=xnb_free[par] + (tmur,), inc=S["v"])
        xt_free[par] = (txn,)
        trr = R.emit("vector", lambda e: e.tensor_mul(
            st["rr"][:, :], st["r"][:, :], st["r"][:, :]), waits=(txn,), inc=S["v"])
        tu = R.emit("vector", lambda e: e.tensor_mul(st["u"][:, :], st["var"][:, :], st["rr"][:, :]),
                    waits=(trr,), inc=S["v"])
        tsd2 = R.emit("scalar", lambda e: e.activation(
            st["sd2"][:, :], st["u"][:, :], AF.Sqrt, bias=sb["epscol"][:, :]),
            waits=(tu,), inc=S["s"])
        tm = R.emit("scalar", lambda e: e.activation(
            st["m"][:, :], st["u"][:, :], AF.Sqrt, scale=float(D)), waits=(tu,), inc=S["s"])
        tss = R.emit("vector", lambda e: e.reciprocal(st["s"][:, :], st["sd2"][:, :]),
                     waits=(tsd2,), inc=S["v"])
        tscc = R.emit("vector", lambda e: e.tensor_scalar_mul(st["sc"][:, :], st["s"][:, :], CQK),
                      waits=(tss,), inc=S["v"])
        R.emit("vector", lambda e, par=par: e.tensor_copy(smtile[par][:, 0:1], st["sc"][:, :]),
               waits=(tscc,) + smt_free[par])
        R.emit("vector", lambda e, par=par: e.tensor_copy(smtile[par][:, 1:2], st["s"][:, :]))
        R.emit("vector", lambda e, par=par: e.tensor_copy(smtile[par][:, 2:3], st["m"][:, :]),
               waits=(tm,))
        tsm = R.emit("vector", lambda e, par=par: e.memset(smtile[par][:, 3:4], 0.0), inc=S["v"])
        tst = R.emit("sync", lambda e, t=t, par=par: e.dma_start(
            out=sm_dst[t * 128:(t + 1) * 128, :], in_=smtile[par][:, :]),
            waits=(tsm,), inc=S["dsy"], inck=16)
        smt_free[par] = (tst,)
        smst_toks.append(tst)

        xnst = None
        if "xn_dst" in io:
            xnst = R.emit("sync", lambda e, t=t, par=par: e.dma_start(
                out=io["xn_dst"][t * 128:(t + 1) * 128, :], in_=xnb[par][:, :]),
                waits=(txn,), inc=S["dsy"], inck=16)
            smst_toks.append(xnst)
        tp_toks = []
        for kc in range(NKC):
            tt = R.emit("tensor", lambda e, kc=kc, par=par: e.transpose(
                ps_xt[:, kc * 128:(kc + 1) * 128], xnb[par][:, kc * 128:(kc + 1) * 128],
                identb[:, :]),
                waits=(txn, t_gw) + (xn_toks[-1][1:] if xn_toks else ()),
                inc=S["t"] if kc == NKC - 1 else None)
            if tt is not None:
                tp_toks.append(tt)
        xnb_free[par] = tuple(tp_toks) + ((xnst,) if xnst is not None else ())
        tev = R.emit("vector", lambda e, t=t: e.tensor_copy(
            xnT_my[:, :, t * 128:(t + 1) * 128],
            ps_xt[:, :].rearrange("r (k tt) -> r k tt", tt=128)),
            waits=tuple(tp_toks), inc=S["v"])
        xn_toks.append((tev, tev))

        gmm = None
        for kc in range(NKC):
            gmm = R.emit("tensor", lambda e, kc=kc, t=t: e.matmul(
                ps_g[:, :], xnT_my[:, kc, t * 128:(t + 1) * 128], gw_sb[:, kc, :],
                start=(kc == 0), stop=(kc == NKC - 1)),
                waits=(tev,) + ps_g_free,
                inc=S["t"] if kc == NKC - 1 else None)
        tsg = R.emit("scalar", lambda e: e.activation(sg[:, :], ps_g[:, :], AF.Sigmoid),
                     waits=(gmm,), inc=S["s"])
        ps_g_free = (tsg,)
        def vch(fn, w, tk):
            return R.emit("vector", fn, waits=w + ((tk,) if tk else ()), inc=S["v"])
        tk = vch(lambda e: e.reduce_max(
            gm[:, :], sg[:, :].rearrange("r (g e2) -> r g e2", e2=4), axis=AX.X),
            (tsg,), None)
        tk = vch(lambda e: e.reduce_max(m1[:, :], gm[:, :], axis=AX.X), (), tk)
        tk = vch(lambda e: e.tensor_single_scalar(eq[:, :], gm[:, :], m1[:, :], ALU.is_ge), (), tk)
        tk = vch(lambda e: e.scalar_tensor_tensor(
            gm2[:, :], eq[:, :], -1e30, gm[:, :], ALU.mult, ALU.add), (), tk)
        tk = vch(lambda e: e.reduce_max(m1[:, :], gm2[:, :], axis=AX.X), (), tk)
        tk = vch(lambda e: e.tensor_single_scalar(eq[:, :], gm[:, :], m1[:, :], ALU.is_ge), (), tk)
        for e0 in range(4):
            tk = vch(lambda e, e0=e0: e.tensor_copy(
                emask[:, :].rearrange("r (g e2) -> r g e2", e2=4)[:, :, e0], eq[:, :]),
                (), tk if e0 == 0 else None)
        tk = vch(lambda e: e.tensor_mul(ms0[:, :], sg[:, :], emask[:, :]), (), tk)
        tk = vch(lambda e: e.tensor_copy(msx[:, :], ms0[:, :]), (), tk)
        for _ in range(3):
            tk = vch(lambda e: e.reduce_max(m1[:, :], msx[:, :], axis=AX.X), (), tk)
            tk = vch(lambda e: e.tensor_single_scalar(sel[:, :], msx[:, :], m1[:, :], ALU.is_ge), (), tk)
            tk = vch(lambda e: e.scalar_tensor_tensor(
                msx[:, :], sel[:, :], -1e30, msx[:, :], ALU.mult, ALU.add), (), tk)
        tk = vch(lambda e: e.reduce_max(m1[:, :], msx[:, :], axis=AX.X), (), tk)
        tk = vch(lambda e: e.tensor_single_scalar(sel[:, :], ms0[:, :], m1[:, :], ALU.is_ge), (), tk)
        tk = vch(lambda e: e.tensor_mul(wv_[:, :], sg[:, :], sel[:, :]), (), tk)
        tk = vch(lambda e: e.reduce_sum(wsum[:, :], wv_[:, :], axis=AX.X), (), tk)
        tk = vch(lambda e: e.reciprocal(winv[:, :], wsum[:, :]), (), tk)
        tcmb = vch(lambda e: e.tensor_scalar_mul(cmb[:, :], wv_[:, :], winv[:, :]), (), tk)
        tct = R.emit("tensor", lambda e: e.transpose(
            ps_ct[:, :], cmb[:, :], identf[:, :]), waits=(tcmb,) + ps_ct_free, inc=S["t"])
        tce = R.emit("vector", lambda e, t=t: e.tensor_copy(
            cmbT_st[:, t * 128:(t + 1) * 128], ps_ct[:, :]), waits=(tct,), inc=S["v"])
        ps_ct_free = (tce,)
        gate_toks.append(tce)

    tx1 = R.emit("sync", lambda e: e.dma_start(out=xnT_dst[:, :], in_=xnT_my[:, :, :].rearrange(
        "r k t -> r (k t)")), waits=tuple(x[0] for x in xn_toks), inc=S["dsy"], inck=16)
    tx2 = R.emit("sync", lambda e: e.dma_start(out=cmbT_dst[:, :], in_=cmbT_st[:, :]),
                 waits=tuple(gate_toks), inc=S["dsy"], inck=16)
    return {"stores": (tx1, tx2) + tuple(smst_toks), "xnT_my": xnT_my}



def phase23(kk, R, S, io, sb, sco, sco_outer=None):
    nc = kk.nc
    xnT_full = io["xnT_full"]
    sm_full = io["sm_full"]
    aT_dst = io["aT_dst"]
    identb, identf = sb["identb"], sb["identf"]
    onesf = sb["onesf"]

    qnT = [kk.sb(f"qnT{i}", [128, C], BF16, sco_outer) for i in range(4)]
    knT = [kk.sb(f"knT{i}", [128, C], BF16, sco_outer) for i in range(4)]
    vex = [kk.sb(f"vex{i}", [128, 32, 129], BF16, sco_outer) for i in range(4)]
    negC = kk.sb("negC", [128, 1], F32, sco_outer)
    cos2 = kk.sb("cos2_sb", [128, 32, 128], F32, sco)
    sin2 = kk.sb("sin2_sb", [128, 32, 128], F32, sco)
    raw_q = kk.sb("rawq", [128, NKC, 256], BF16, sco)
    raw_k = kk.sb("rawk", [128, NKC, 256], BF16, sco)
    wq_g = kk.sb("wqg", [128, NKC, 256], BF16, sco)
    wk_g = kk.sb("wkg", [128, NKC, 256], BF16, sco)
    wv_g = kk.sb("wvg", [128, NKC, 256], BF16, sco)
    qg_col = kk.sb("qgc", [128, NKC], F32, sco)
    kvg_col = kk.sb("kvgc", [128, NKC], F32, sco)
    qb_col = kk.sb("qbc", [128, NKC], BF16, sco)
    kvb_col = kk.sb("kvbc", [128, NKC], BF16, sco)
    xg = [kk.sb("xg0", [128, NKC, 128], BF16, sco), kk.sb("xg1", [128, NKC, 128], BF16, sco)]
    smt = [kk.sb("smt0", [128, 4], F32, sco), kk.sb("smt1", [128, 4], F32, sco)]
    browq = kk.sb("browq", [128, 256], F32, sco)
    browk = kk.sb("browk", [128, 256], F32, sco)
    browv = kk.sb("browv", [128, 256], F32, sco)
    g_bc = kk.sb("g_bc", [128, 128], F32, sco)
    b_bc = kk.sb("b_bc", [128, 128], F32, sco)
    brow_sc = kk.sb("brsc", [1, 256], F32, sco)
    brow_sc2 = kk.sb("brsc2", [1, 256], F32, sco)
    lngr = kk.sb("lngr", [1, 128], F32, sco)
    lnbr = kk.sb("lnbr", [1, 128], F32, sco)
    bqr = kk.sb("bqr", [1, 256], F32, sco)
    bkr = kk.sb("bkr", [1, 256], F32, sco)
    bvr = kk.sb("bvr", [1, 256], F32, sco)
    scr128 = kk.sb("scr128", [1, 128], F32, sco)
    sc1 = kk.sb("sc1", [1, 1], F32, sco)
    sc2 = kk.sb("sc2", [1, 1], F32, sco)
    sc3 = kk.sb("sc3", [1, 1], F32, sco)
    qke = kk.sb("qke", [128, 4, 128], F32, sco)
    rot = kk.sb("rot", [128, 4, 128], F32, sco)
    tt1 = kk.sb("tt1", [128, 128], F32, sco)
    tt2 = kk.sb("tt2", [128, 128], F32, sco)
    tt3 = kk.sb("tt3", [128, 128], F32, sco)
    tt4 = kk.sb("tt4", [128, 128], F32, sco)
    sq2 = kk.sb("sq2", [128, 128], F32, sco)
    tq_ = [kk.sb(f"tq{i}", [128, 128], F32, sco) for i in range(4)]
    C2_ = [kk.sb(f"C2{i}", [128, 128], F32, sco) for i in range(4)]
    qn_st = [kk.sb(f"qnst{i}", [128, 128], BF16, sco) for i in range(4)]
    lst = {n: kk.sb("p2_" + n, [128, 4], F32, sco) for n in
           ("su", "ssu", "muu", "e2u", "mu", "e2", "mu2", "var",
            "sd", "rho", "k1", "mrh", "nmr")}
    msqc = kk.sb("p2_msqc", [128, 1], F32, sco)

    ps_q = [kk.ps(f"psq{i}", [128, 256], F32, sco) for i in range(2)]
    ps_k = [kk.ps(f"psk{i}", [128, 256], F32, sco) for i in range(2)]
    ps_v = [kk.ps(f"psv{i}", [128, 256], F32, sco) for i in range(2)]
    ps_m = kk.ps("psm", [128, 256], F32, sco)
    ps_t4 = kk.ps("pst4", [128, 512], BF16, sco)

    wq, wk, wv = io["wq"], io["wk"], io["wv"]
    bq, bk, bv = io["bq"], io["bk"], io["bv"]
    qn_g, qn_b = io["qn_g"], io["qn_b"]
    kvn_g, kvn_b = io["kvn_g"], io["kvn_b"]
    ln_g, ln_b = io["ln_g"], io["ln_b"]
    cos2d, sin2d = io["cos2"], io["sin2"]

    tload = []
    tload.append(R.emit("gpsimd", lambda e: e.dma_start(
        out=raw_q[:, :, :], in_=_rearr_kr(wq.ap())), inc=S["dg"], inck=16))
    tload.append(R.emit("gpsimd", lambda e: e.dma_start(
        out=raw_k[:, :, :], in_=_rearr_kr(wk.ap())), inc=S["dg"], inck=16))
    tload.append(R.emit("gpsimd", lambda e: e.dma_start(
        out=qb_col[:, :], in_=qn_b.ap().rearrange("(k r) -> r k", r=128),
        allow_slow_non_contiguous=True), inc=S["dg"], inck=16))
    tload.append(R.emit("gpsimd", lambda e: e.dma_start(
        out=kvb_col[:, :], in_=kvn_b.ap().rearrange("(k r) -> r k", r=128),
        allow_slow_non_contiguous=True), inc=S["dg"], inck=16))
    for nm, dst, src in (("qg", qg_col, qn_g), ("kvg", kvg_col, kvn_g)):
        tload.append(R.emit("sync", lambda e, dst=dst, src=src: e.dma_start(
            out=dst[:, :], in_=src.ap().rearrange("(k r) -> r k", r=128),
            allow_slow_non_contiguous=True), inc=S["dsy"], inck=16))
    for dst, src in ((cos2, cos2d), (sin2, sin2d)):
        tload.append(R.emit("sync", lambda e, dst=dst, src=src: e.dma_start(
            out=dst[:, :, :], in_=_rearr_kr(src.ap())), inc=S["dsy"], inck=16))
    for dst, src in ((lngr, ln_g), (lnbr, ln_b), (bqr, bq), (bkr, bk), (bvr, bv)):
        tload.append(R.emit("sync", lambda e, dst=dst, src=src: e.dma_start(
            out=dst[:, :], in_=src[:, :]), inc=S["dsy"], inck=16))
    tld = tuple(tload)

    def make_brow(name, raw, bcol, brow_row, scale, dst_bc, prev):
        mmt = None
        for kc in range(NKC):
            mmt = R.emit("tensor", lambda e, kc=kc, raw=raw, bcol=bcol: e.matmul(
                ps_m[0:1, :], bcol[:, kc:kc + 1], raw[:, kc, :],
                start=(kc == 0), stop=(kc == NKC - 1)),
                waits=tld + prev, inc=S["t"] if kc == NKC - 1 else None)
        t1_ = R.emit("vector", lambda e: e.tensor_copy(brow_sc[0:1, :], ps_m[0:1, :]),
                     waits=(mmt,), inc=S["v"])
        t2_ = R.emit("vector", lambda e, brow_row=brow_row: e.tensor_add(
            brow_sc2[0:1, :], brow_sc[0:1, :], brow_row[0:1, :]), waits=(t1_,), inc=S["v"])
        t3_ = R.emit("vector", lambda e, scale=scale: e.tensor_scalar_mul(
            brow_sc[0:1, :], brow_sc2[0:1, :], scale), waits=(t2_,), inc=S["v"])
        bct = R.emit("tensor", lambda e: e.matmul(
            ps_m[:, :], onesf[0:1, :], brow_sc[0:1, :], start=True, stop=True),
            waits=(t3_,), inc=S["t"])
        evt = R.emit("vector", lambda e, dst_bc=dst_bc: e.tensor_copy(
            dst_bc[:, :], ps_m[:, :]), waits=(bct,), inc=S["v"])
        return (evt,)

    pv = make_brow("q", raw_q, qb_col, bqr, CQK, browq, ())
    pv = make_brow("k", raw_k, kvb_col, bkr, CQK, browk, pv)
    wsc = []
    for kc in range(NKC):
        wsc.append(R.emit("vector", lambda e, kc=kc: e.tensor_scalar_mul(
            wq_g[:, kc, :], raw_q[:, kc, :], qg_col[:, kc:kc + 1]),
            waits=tld + pv, inc=S["v"] if kc == NKC - 1 else None))
        wsc.append(R.emit("vector", lambda e, kc=kc: e.tensor_scalar_mul(
            wk_g[:, kc, :], raw_k[:, kc, :], kvg_col[:, kc:kc + 1]),
            inc=S["v"] if kc == NKC - 1 else None))
    wsc = tuple(x for x in wsc if x is not None)
    tldv = R.emit("gpsimd", lambda e: e.dma_start(
        out=raw_q[:, :, :], in_=_rearr_kr(wv.ap())), waits=pv + wsc, inc=S["dg"], inck=16)
    pv2 = make_brow("v", raw_q, kvb_col, bvr, 1.0, browv, (tldv,))
    wscv = []
    for kc in range(NKC):
        wscv.append(R.emit("vector", lambda e, kc=kc: e.tensor_scalar_mul(
            wv_g[:, kc, :], raw_q[:, kc, :], kvg_col[:, kc:kc + 1]),
            waits=(tldv,) + pv2, inc=S["v"] if kc == NKC - 1 else None))
    w_ready = wsc + pv2 + tuple(x for x in wscv if x is not None)

    tc0 = R.emit("vector", lambda e: e.reduce_max(sc1[:, :], lngr[0:1, :], axis=AX.X,
                                                  apply_absolute_value=True),
                 waits=tld, inc=S["v"])
    tb1 = R.emit("vector", lambda e: e.scalar_tensor_tensor(
        scr128[0:1, :], lnbr[0:1, :], 1.0, lnbr[0:1, :], ALU.mult, ALU.mult,
        accum_out=sc2[:, :]), inc=S["v"])
    tb2 = R.emit("scalar", lambda e: e.activation(sc2[:, :], sc2[:, :], AF.Sqrt),
                 waits=(tb1,), inc=S["s"])
    tc1 = R.emit("vector", lambda e: e.tensor_scalar_mul(sc1[:, :], sc1[:, :], SQ128),
                 waits=(tb2, tc0), inc=S["v"])
    tc2 = R.emit("vector", lambda e: e.tensor_add(sc3[:, :], sc1[:, :], sc2[:, :]),
                 waits=(tc1,), inc=S["v"])
    tb3 = R.emit("scalar", lambda e: e.activation(sc3[:, :], sc3[:, :], AF.Square),
                 waits=(tc2,), inc=S["s"])
    tb4 = R.emit("vector", lambda e: e.tensor_scalar_mul(sc3[:, :], sc3[:, :], -ISQH),
                 waits=(tb3,), inc=S["v"])
    tb5 = R.emit("tensor", lambda e: e.matmul(
        ps_m[:, 0:1], onesf[0:1, :], sc3[0:1, 0:1], start=True, stop=True),
        waits=(tb4,) + pv2, inc=S["t"])
    tnegC = R.emit("vector", lambda e: e.tensor_copy(negC[:, :], ps_m[:, 0:1]),
                   waits=(tb5,), inc=S["v"])
    tb6 = R.emit("tensor", lambda e: e.matmul(
        ps_m[:, 0:128], onesf[0:1, :], lngr[0:1, :], start=True, stop=True),
        waits=(tnegC,), inc=S["t"])
    tgbc = R.emit("vector", lambda e: e.tensor_copy(g_bc[:, :], ps_m[:, 0:128]),
                  waits=(tb6,), inc=S["v"])
    tb7 = R.emit("tensor", lambda e: e.matmul(
        ps_m[:, 0:128], onesf[0:1, :], lnbr[0:1, :], start=True, stop=True),
        waits=(tgbc,), inc=S["t"])
    tbbc = R.emit("vector", lambda e: e.tensor_copy(b_bc[:, :], ps_m[:, 0:128]),
                  waits=(tb7,), inc=S["v"])
    prep_done = w_ready + (tnegC, tgbc, tbbc)

    vex_ones = []
    for bh in range(4):
        vex_ones.append(R.emit("vector", lambda e, bh=bh: e.memset(
            vex[bh][:, :, 128:129], 1.0), inc=S["v"]))
    vex_ones = tuple(vex_ones)

    xg_free = [(), ()]
    psq_free = [(), ()]
    psk_free = [(), ()]
    psv_free = [(), ()]
    smt_free = [(), ()]
    qnst_free = [()] * 4
    ssu_free = ()
    pst4_free = ()
    fill_toks = []

    for g in range(T // 128):
        par = g % 2
        b = g // 32
        p = g % 32
        rb = g // 8
        tl = g % 8
        txg = R.emit("sync", lambda e, par=par, rb=rb, tl=tl: e.dma_start(
            out=xg[par][:, :, :],
            in_=xnT_full[128 * rb:128 * (rb + 1), :]
            .rearrange("r (k tl2 tt) -> r k tl2 tt", k=NKC, tl2=8)[:, :, tl, :]),
            waits=xg_free[par] + io.get("ag_gate", ()), inc=S["dsy"], inck=16)
        tsm_ = R.emit("sync", lambda e, par=par, g=g: e.dma_start(
            out=smt[par][:, :], in_=sm_full[g * 128:(g + 1) * 128, :]),
            waits=smt_free[par] + io.get("ag_gate", ()), inc=S["dsy"], inck=16)

        mmq = mmk = mmv = None
        for kc in range(NKC):
            mmq = R.emit("tensor", lambda e, kc=kc, par=par: e.matmul(
                ps_q[par][:, :], xg[par][:, kc, :], wq_g[:, kc, :],
                start=(kc == 0), stop=(kc == NKC - 1)),
                waits=(txg,) + prep_done + psq_free[par],
                inc=S["t"] if kc == NKC - 1 else None)
        for kc in range(NKC):
            mmk = R.emit("tensor", lambda e, kc=kc, par=par: e.matmul(
                ps_k[par][:, :], xg[par][:, kc, :], wk_g[:, kc, :],
                start=(kc == 0), stop=(kc == NKC - 1)),
                waits=psk_free[par], inc=S["t"] if kc == NKC - 1 else None)
        for kc in range(NKC):
            mmv = R.emit("tensor", lambda e, kc=kc, par=par: e.matmul(
                ps_v[par][:, :], xg[par][:, kc, :], wv_g[:, kc, :],
                start=(kc == 0), stop=(kc == NKC - 1)),
                waits=psv_free[par], inc=S["t"] if kc == NKC - 1 else None)
        xg_free[par] = (mmv,)

        tqe = R.emit("vector", lambda e, par=par: e.scalar_tensor_tensor(
            qke[:, 0:2, :].rearrange("r l x -> r (l x)"), ps_q[par][:, :],
            smt[par][:, 0:1], browq[:, :], ALU.mult, ALU.add),
            waits=(mmq, tsm_), inc=S["v"])
        psq_free[par] = (tqe,)
        tke = R.emit("vector", lambda e, par=par: e.scalar_tensor_tensor(
            qke[:, 2:4, :].rearrange("r l x -> r (l x)"), ps_k[par][:, :],
            smt[par][:, 0:1], browk[:, :], ALU.mult, ALU.add),
            waits=(mmk,), inc=S["v"])
        psk_free[par] = (tke,)
        tv0 = R.emit("vector", lambda e, par=par, b=b, p=p: e.scalar_tensor_tensor(
            vex[2 * b][:, p, 0:128], ps_v[par][:, 0:128], smt[par][:, 1:2],
            browv[:, 0:128], ALU.mult, ALU.add), waits=(mmv,) + vex_ones)
        tv1 = R.emit("vector", lambda e, par=par, b=b, p=p: e.scalar_tensor_tensor(
            vex[2 * b + 1][:, p, 0:128], ps_v[par][:, 128:256], smt[par][:, 1:2],
            browv[:, 128:256], ALU.mult, ALU.add), inc=S["v"])
        psv_free[par] = (tv1,)
        fill_toks.append(tv1)

        v2 = lambda f, w=(), i=None: R.emit("vector", f, waits=w, inc=i)
        cv = cos2[:, p, :].rearrange("r (hl j) -> r hl j", j=64)
        sv = sin2[:, p, :].rearrange("r (hl j) -> r hl j", j=64)
        trot = None
        for si, t_src in ((0, tqe), (2, tke)):
            a_ap = qke[:, si:si + 2, 0:64]
            b_ap = qke[:, si:si + 2, 64:128]
            oa = rot[:, si:si + 2, 0:64]
            ob = rot[:, si:si + 2, 64:128]
            v2(lambda e, a_ap=a_ap, cv=cv: e.tensor_tensor(
                tt1[:, :].rearrange("r (hl j) -> r hl j", j=64), a_ap, cv, ALU.mult),
               w=(t_src,))
            v2(lambda e, b_ap=b_ap, sv=sv: e.tensor_tensor(
                tt2[:, :].rearrange("r (hl j) -> r hl j", j=64), b_ap, sv, ALU.mult))
            v2(lambda e, a_ap=a_ap, sv=sv: e.tensor_tensor(
                tt3[:, :].rearrange("r (hl j) -> r hl j", j=64), a_ap, sv, ALU.mult))
            t4tok = v2(lambda e, b_ap=b_ap, cv=cv: e.tensor_tensor(
                tt4[:, :].rearrange("r (hl j) -> r hl j", j=64), b_ap, cv, ALU.mult),
                i=S["v"])
            v2(lambda e, oa=oa: e.tensor_tensor(
                oa, tt1[:, :].rearrange("r (hl j) -> r hl j", j=64),
                tt2[:, :].rearrange("r (hl j) -> r hl j", j=64), ALU.subtract),
               w=(t4tok,))
            trot = v2(lambda e, ob=ob: e.tensor_tensor(
                ob, tt3[:, :].rearrange("r (hl j) -> r hl j", j=64),
                tt4[:, :].rearrange("r (hl j) -> r hl j", j=64), ALU.add), i=S["v"])

        tsus = []
        for l in range(4):
            tsus.append(R.emit("vector", lambda e, l=l: e.reduce_sum(
                lst["su"][:, l:l + 1], rot[:, l, :], axis=AX.X),
                waits=(trot,), inc=S["v"] if l == 3 else None))
        tsu4 = tsus[-1]
        tsss = []
        for l in range(4):
            tsss.append(R.emit("scalar", lambda e, l=l: e.activation(
                sq2[:, :], rot[:, l, :], AF.Square, accum_out=lst["ssu"][:, l:l + 1]),
                waits=(trot,) + ssu_free, inc=S["s"] if l == 3 else None))
        tss4 = tsss[-1]
        tk = v2(lambda e: e.tensor_scalar_mul(lst["muu"][:, :], lst["su"][:, :], 1.0 / 128),
                w=(tsu4,), i=S["v"])
        te2u = v2(lambda e: e.tensor_scalar_mul(lst["e2u"][:, :], lst["ssu"][:, :], 1.0 / 128),
                  w=(tss4, tk), i=S["v"])
        ssu_free = (te2u,)
        tk = v2(lambda e, par=par: e.tensor_mul(msqc[:, :], smt[par][:, 2:3],
                                                smt[par][:, 2:3]), w=(te2u,), i=S["v"])
        tk = v2(lambda e, par=par: e.tensor_scalar_mul(
            lst["mu"][:, :], lst["muu"][:, :], smt[par][:, 2:3]), w=(tk,), i=S["v"])
        tk = v2(lambda e: e.tensor_scalar_mul(lst["e2"][:, :], lst["e2u"][:, :],
                                              msqc[:, :]), w=(tk,), i=S["v"])
        tk = v2(lambda e: e.tensor_mul(lst["mu2"][:, :], lst["mu"][:, :], lst["mu"][:, :]),
                w=(tk,), i=S["v"])
        tvr = v2(lambda e: e.tensor_sub(lst["var"][:, :], lst["e2"][:, :], lst["mu2"][:, :]),
                 w=(tk,), i=S["v"])
        tsd3 = R.emit("scalar", lambda e: e.activation(
            lst["sd"][:, :], lst["var"][:, :], AF.Sqrt, bias=sb["epscol"][:, :]),
            waits=(tvr,), inc=S["s"])
        tk = v2(lambda e: e.reciprocal(lst["rho"][:, :], lst["sd"][:, :]),
                w=(tsd3,), i=S["v"])
        tk = v2(lambda e, par=par: e.tensor_scalar_mul(
            lst["k1"][:, :], lst["rho"][:, :], smt[par][:, 2:3]), w=(tk,), i=S["v"])
        tk = v2(lambda e: e.tensor_mul(lst["mrh"][:, :], lst["mu"][:, :], lst["rho"][:, :]),
                w=(tk,), i=S["v"])
        tchain = v2(lambda e: e.tensor_scalar_mul(lst["nmr"][:, :], lst["mrh"][:, :], -1.0),
                    w=(tk,), i=S["v"])
        smt_free[par] = (tchain,)

        tqts = []
        for l in range(4):
            tqts.append(v2(lambda e, l=l: e.scalar_tensor_tensor(
                tq_[l][:, :], rot[:, l, :], lst["k1"][:, l:l + 1], g_bc[:, :],
                ALU.mult, ALU.mult), w=(tchain,),
                i=S["v"] if l == 3 else None))
        for l in range(4):
            tqts.append(v2(lambda e, l=l: e.scalar_tensor_tensor(
                C2_[l][:, :], g_bc[:, :], lst["nmr"][:, l:l + 1], b_bc[:, :],
                ALU.mult, ALU.add), i=S["v"] if l == 3 else None))
        tC4 = tqts[-1]
        for l in range(4):
            slot = l
            bh = 2 * b + (l % 2)
            dstbuf = qnT if l < 2 else knT
            tqn = v2(lambda e, l=l: e.tensor_add(
                qn_st[l][:, :], tq_[l][:, :], C2_[l][:, :]),
                w=(tC4,) + qnst_free[l], i=S["v"])
            ttp = R.emit("tensor", lambda e, slot=slot, l=l: e.transpose(
                ps_t4[:, slot * 128:(slot + 1) * 128], qn_st[slot][:, :], identb[:, :]),
                waits=(tqn,) + pst4_free, inc=S["t"])
            tev2 = R.emit("vector", lambda e, dstbuf=dstbuf, bh=bh, p=p, slot=slot:
                          e.tensor_copy(dstbuf[bh][:, p * 128:(p + 1) * 128],
                                        ps_t4[:, slot * 128:(slot + 1) * 128]),
                          waits=(ttp,), inc=S["v"])
            qnst_free[l] = (ttp,)
            fill_toks.append(tev2)
        pst4_free = (fill_toks[-1],)

    return {"fill": tuple(fill_toks[-16:]), "qnT": qnT, "knT": knT, "vex": vex,
            "negC": negC}


def phase_attn(kk, R, S, io, sb, p2, sco):
    nc = kk.nc
    aT_dst = io["aT_dst"]
    identb = sb["identb"]
    qnT, knT, vex, negC = p2["qnT"], p2["knT"], p2["vex"], p2["negC"]
    fill = p2["fill"]

    ptb = kk.sb("ptb", [128, 32, 512], BF16, sco)
    diag_b = kk.sb("diagb", [128, 128], BF16, sco)
    rz = kk.sb("rz", [128, 1], F32, sco)
    a_st = [kk.sb(f"ast{i}", [128, 128], BF16, sco) for i in range(2)]
    at_st = [kk.sb(f"atst{i}", [128, 512], BF16, sco) for i in range(2)]

    ps_s = [kk.ps(f"pss{i}", [128, 512], F32, sco) for i in range(2)]
    ps_av = [kk.ps(f"psav{i}", [128, 256], F32, sco) for i in range(2)]
    ps_at = kk.ps("psat", [128, 512], BF16, sco)

    tdg = R.emit("gpsimd", lambda e: e.dma_start(
        out=diag_b[:, :], in_=io["diagmask"][:, :]), inc=S["dg"], inck=16)

    store_toks = []
    at_free = [(), ()]
    atst_free = [(), ()]
    for bh in range(4):
        b, hl = bh // 2, bh % 2
        pss_free = [(), ()]
        psav_free = [(), ()]
        prev_av_done = ()
        for qb in range(8):
            nkt = 4 * qb + 4
            exp_toks = []
            for kt in range(nkt):
                spar = kt % 2
                mm = R.emit("tensor", lambda e, bh=bh, kt=kt, qb=qb, spar=spar: e.matmul(
                    ps_s[spar][:, :], knT[bh][:, kt * 128:(kt + 1) * 128],
                    qnT[bh][:, qb * 512:(qb + 1) * 512], start=True, stop=True),
                    waits=fill + pss_free[spar] + prev_av_done, inc=S["t"])
                jj = kt - 4 * qb
                if jj < 0:
                    te = R.emit("scalar", lambda e, kt=kt, spar=spar: e.activation(
                        ptb[:, kt, :], ps_s[spar][:, :], AF.Exp,
                        bias=negC[:, :], scale=ISQH), waits=(mm, tdg), inc=S["s"])
                    pss_free[spar] = (te,)
                    exp_toks.append(te)
                else:
                    if jj > 0:
                        tz = R.emit("vector", lambda e, kt=kt, jj=jj: e.memset(
                            ptb[:, kt, 0:jj * 128], 0.0), waits=prev_av_done, inc=S["v"])
                        exp_toks.append(tz)
                    te = R.emit("scalar", lambda e, kt=kt, jj=jj, spar=spar: e.activation(
                        ptb[:, kt, jj * 128:512], ps_s[spar][:, jj * 128:512], AF.Exp,
                        bias=negC[:, :], scale=ISQH), waits=(mm, tdg), inc=S["s"])
                    pss_free[spar] = (te,)
                    tmk = R.emit("vector", lambda e, kt=kt, jj=jj: e.tensor_tensor(
                        ptb[:, kt, jj * 128:(jj + 1) * 128],
                        ptb[:, kt, jj * 128:(jj + 1) * 128], diag_b[:, :], ALU.mult),
                        waits=(te,), inc=S["v"])
                    exp_toks.append(tmk)
            exp_toks = tuple(exp_toks)

            av_last = []
            for j in range(4):
                qt = 4 * qb + j
                apar = j % 2
                mm = None
                for kt in range(qt + 1):
                    mm = R.emit("tensor", lambda e, bh=bh, kt=kt, j=j, qt=qt, apar=apar:
                                e.matmul(ps_av[apar][:, 0:129],
                                         ptb[:, kt, j * 128:(j + 1) * 128],
                                         vex[bh][:, kt, :],
                                         start=(kt == 0), stop=(kt == qt)),
                                waits=exp_toks + psav_free[apar],
                                inc=S["t"] if kt == qt else None)
                av_last.append(mm)
                trz = R.emit("vector", lambda e, apar=apar: e.reciprocal(
                    rz[:, :], ps_av[apar][:, 128:129]), waits=(mm,), inc=S["v"])
                tas = R.emit("vector", lambda e, apar=apar: e.tensor_scalar_mul(
                    a_st[apar][:, :], ps_av[apar][:, 0:128], rz[:, :]),
                    waits=(trz,), inc=S["v"])
                psav_free[apar] = (tas,)
                ttp = R.emit("tensor", lambda e, j=j, apar=apar: e.transpose(
                    ps_at[:, j * 128:(j + 1) * 128], a_st[apar][:, :], identb[:, :]),
                    waits=(tas,) + (at_free[0] if j == 0 else ()), inc=S["t"])
                av_last.append(ttp)
            tev = R.emit("vector", lambda e, qb=qb: e.tensor_copy(
                at_st[qb % 2][:, :], ps_at[:, :]),
                waits=tuple(x for x in av_last if x) + atst_free[qb % 2], inc=S["v"])
            tstore = R.emit("sync", lambda e, b=b, hl=hl, qb=qb: e.dma_start(
                out=aT_dst[hl * 256 + b * 128:hl * 256 + b * 128 + 128,
                           qb * 512:(qb + 1) * 512], in_=at_st[qb % 2][:, :]),
                waits=(tev,), inc=S["dsy"], inck=16)
            at_free[0] = (tev,)
            atst_free[qb % 2] = (tstore,)
            store_toks.append(tstore)
            prev_av_done = tuple(x for x in av_last if x is not None)
    return {"stores": tuple(store_toks)}



def phase4(kk, R, S, io, sb, H, sco):
    nc = kk.nc
    xnT_full = io["xnT_full"]
    cmbT_full = io["cmbT_full"]
    w1t, w3t, w2t = io["w1t"], io["w3t"], io["w2t"]
    y_dst = io["y_dst"]
    identb = sb["identb"]

    xg = [kk.sb(f"m_xg{i}", [128, NKC, 512], BF16, sco) for i in range(4)]
    glu = [kk.sb(f"m_glu{i}", [128, NIC, 2048], BF16, sco) for i in range(2)]
    w1b = [kk.sb(f"m_w1{i}", [128, NKC, 128], BF16, sco) for i in range(2)]
    w3b = [kk.sb(f"m_w3{i}", [128, NKC, 128], BF16, sco) for i in range(2)]
    cmbp = kk.sb("m_cmbp", [2, 2, 1024], BF16, sco)
    cmbcol = kk.sb("m_cmbcol", [128, 16, 2], F32, sco)
    s1 = [kk.sb(f"m_s1{i}", [128, 512], F32, sco) for i in range(2)]
    w2b = [kk.sb(f"m_w2{i}", [128, NIC, 512], BF16, sco) for i in range(2)]
    yste = [kk.sb(f"m_ye{i}", [128, 512], F32, sco) for i in range(2)]
    yst = [kk.sb(f"m_yst{i}", [128, 512], BF16, sco) for i in range(2)]

    ps_h1 = [kk.ps(f"ph1{i}", [128, 512], F32, sco) for i in range(2)]
    ps_h3 = [kk.ps(f"ph3{i}", [128, 512], F32, sco) for i in range(2)]
    ps_y = [kk.ps(f"py{i}", [128, 512], F32, sco) for i in range(2)]
    ps_ct = kk.ps("pct2", [128, 32], BF16, sco)

    xg_free = [()] * 4
    w1_free = [(), ()]
    w3_free = [(), ()]
    w2_free = [(), ()]
    glu_free = [(), ()]
    cmbp_free = ()
    ps_ct_free = ()
    cmbcol_free = ()
    s1_free = [(), ()]
    h1_free = [(), ()]
    h3_free = [(), ()]
    y_free = [(), ()]
    yste_free = [(), ()]
    yst_free = [(), ()]
    store_toks = []

    for sbi in range(4):
        ct_toks = []
        for j in range(2):
            rblk = sbi * 2 + j
            tcp = R.emit("gpsimd", lambda e, rblk=rblk, j=j: e.dma_start(
                out=cmbp[:, j, :],
                in_=cmbT_full[ds(H["cid"] * 2 + 16 * rblk, 2), :]),
                waits=cmbp_free + io.get("ag_gate", ()),
                inc=S["dg"], inck=16)
            for k in range(8):
                tct = R.emit("tensor", lambda e, j=j, k=k: e.transpose(
                    ps_ct[:, (j * 8 + k) * 2:(j * 8 + k) * 2 + 2],
                    cmbp[0:2, j, k * 128:(k + 1) * 128], identb[0:2, 0:2]),
                    waits=(tcp,) + ps_ct_free,
                    inc=S["t"] if (j, k) == (1, 7) else None)
            if j == 1:
                ct_toks.append(tct)
        tcc = R.emit("vector", lambda e: e.tensor_copy(
            cmbcol[:, :, :], ps_ct[:, :].rearrange("r (t el) -> r t el", el=2)),
            waits=tuple(ct_toks) + cmbcol_free, inc=S["v"])
        ps_ct_free = (tcc,)
        cmbp_free = (ct_toks[-1],)

        xg_toks = []
        for wi in range(4):
            w = sbi * 4 + wi
            rb, half = w // 2, w % 2
            tx = R.emit("sync", lambda e, wi=wi, rb=rb, half=half: e.dma_start(
                out=xg[wi][:, :, :],
                in_=xnT_full[128 * rb:128 * (rb + 1), :]
                .rearrange("r (k tt) -> r k tt", k=NKC)[:, :, half * 512:(half + 1) * 512]),
                waits=xg_free[wi] + io.get("ag_gate", ()), inc=S["dsy"], inck=16)
            xg_toks.append(tx)

        glu_toks = [[], []]
        for el in range(2):
            for ic in range(NIC):
                ipar = ic % 2
                tw1 = R.emit("gpsimd", lambda e, el=el, ic=ic, ipar=ipar: e.dma_start(
                    out=w1b[ipar][:, :, :],
                    in_=_rearr_kr(w1t.ap()[el, :, ic * 128:(ic + 1) * 128])),
                    waits=w1_free[ipar], inc=S["dg"], inck=16)
                tw3 = R.emit("gpsimd", lambda e, el=el, ic=ic, ipar=ipar: e.dma_start(
                    out=w3b[ipar][:, :, :],
                    in_=_rearr_kr(w3t.ap()[el, :, ic * 128:(ic + 1) * 128])),
                    waits=w3_free[ipar], inc=S["dg"], inck=16)
                for wi in range(4):
                    m1_ = m3_ = None
                    for kc in range(NKC):
                        m1_ = R.emit("tensor", lambda e, kc=kc, wi=wi, ipar=ipar: e.matmul(
                            ps_h1[ipar][:, :], w1b[ipar][:, kc, :], xg[wi][:, kc, :],
                            start=(kc == 0), stop=(kc == NKC - 1)),
                            waits=(tw1, xg_toks[wi]) + h1_free[ipar] + glu_free[el],
                            inc=S["t"] if kc == NKC - 1 else None)
                    for kc in range(NKC):
                        m3_ = R.emit("tensor", lambda e, kc=kc, wi=wi, ipar=ipar: e.matmul(
                            ps_h3[ipar][:, :], w3b[ipar][:, kc, :], xg[wi][:, kc, :],
                            start=(kc == 0), stop=(kc == NKC - 1)),
                            waits=(tw3,) + h3_free[ipar],
                            inc=S["t"] if kc == NKC - 1 else None)
                    if wi == 3:
                        w1_free[ipar] = (m1_,)
                        w3_free[ipar] = (m3_,)
                    spp = wi % 2
                    ts1 = R.emit("scalar", lambda e, ipar=ipar, spp=spp: e.activation(
                        s1[spp][:, :], ps_h1[ipar][:, :], AF.Silu),
                        waits=(m1_,) + s1_free[spp], inc=S["s"])
                    h1_free[ipar] = (ts1,)
                    tg1 = R.emit("vector", lambda e, el=el, ic=ic, wi=wi, ipar=ipar,
                                 spp=spp: e.tensor_tensor(
                                     glu[el][:, ic, wi * 512:(wi + 1) * 512],
                                     ps_h3[ipar][:, :], s1[spp][:, :], ALU.mult),
                        waits=(ts1, m3_), inc=S["v"])
                    h3_free[ipar] = (tg1,)
                    s1_free[spp] = (tg1,)
                    glu_toks[el].append(tg1)
        all_glu = (glu_toks[0][-1], glu_toks[1][-1])

        y_sb_toks = []
        for dwq in range(4):
            tw2 = []
            for el in range(2):
                tw2.append(R.emit("gpsimd", lambda e, el=el, dwq=dwq: e.dma_start(
                    out=w2b[el][:, :, :],
                    in_=_rearr_kr(w2t.ap()[el, :, dwq * 512:(dwq + 1) * 512])),
                    waits=w2_free[el], inc=S["dg"], inck=16))
            tw2 = tuple(tw2)
            mmy = None
            for wi in range(4):
                for tt in range(4):
                    tiley = wi * 4 + tt
                    tye = [None, None]
                    for el in range(2):
                        for ic in range(NIC):
                            mmy = R.emit("tensor", lambda e, el=el, ic=ic, wi=wi, tt=tt:
                                         e.matmul(
                                             ps_y[el][:, :],
                                             glu[el][:, ic, wi * 512 + tt * 128:wi * 512 + (tt + 1) * 128],
                                             w2b[el][:, ic, :],
                                             start=(ic == 0), stop=(ic == NIC - 1)),
                                         waits=(tw2[el],) + all_glu + y_free[el],
                                         inc=S["t"] if ic == NIC - 1 else None)
                        tye[el] = R.emit("scalar", lambda e, el=el, tiley=tiley: e.activation(
                            yste[el][:, :], ps_y[el][:, :], AF.Copy,
                            scale=cmbcol[:, tiley, el:el + 1]),
                            waits=(mmy, tcc) + yste_free[el], inc=S["s"])
                        y_free[el] = (tye[el],)
                    tysum = R.emit("vector", lambda e, tiley=tiley: e.tensor_add(
                        yst[tiley % 2][:, :], yste[0][:, :], yste[1][:, :]),
                        waits=(tye[0], tye[1]) + yst_free[tiley % 2], inc=S["v"])
                    yste_free[0] = (tysum,)
                    yste_free[1] = (tysum,)
                    row0 = sbi * 2048 + wi * 512 + tt * 128
                    tys = R.emit("sync", lambda e, row0=row0, dwq=dwq, tiley=tiley: e.dma_start(
                        out=y_dst[row0:row0 + 128, dwq * 512:(dwq + 1) * 512],
                        in_=yst[tiley % 2][:, :]), waits=(tysum,), inc=S["dsy"], inck=16)
                    yst_free[tiley % 2] = (tys,)
                    y_sb_toks.append(tys)
            w2_free[0] = (mmy,)
            w2_free[1] = (mmy,)
        store_toks += y_sb_toks
        for wi in range(4):
            xg_free[wi] = tuple(glu_toks[0][-1:] + glu_toks[1][-1:])
        glu_free = [tuple(y_sb_toks[-1:]), tuple(y_sb_toks[-1:])]
        cmbcol_free = (tye[0], tye[1])
    return {"stores": tuple(store_toks)}


def phase4_sparse(kk, R, S, io, sb, H, sco):
    nc = kk.nc
    xn_full = io["xn_full"]
    cmbT_full = io["cmbT_full"]
    w1t, w3t, w2t = io["w1t"], io["w3t"], io["w2t"]
    y_dst = io["y_dst"]
    identb = sb["identb"]
    identf = sb["identf"]
    onesf = sb["onesf"]
    gate = io.get("ag_gate", ())

    NSB = 2
    CHG = NCH // NSB
    TOKG = CHG * 128
    NW = TOKG // 512

    cmbp = kk.sb("sp_cmbp", [2, 1024], F32, sco)
    selsrc = kk.sb("sp_sel", [128, 64], F32, sco)
    selm = kk.sb("sp_selm", [128, 64], F32, sco)
    pos = kk.sb("sp_pos", [128, 64], F32, sco)
    post = kk.sb("sp_post", [128, 64], F32, sco)
    posi = kk.sb("sp_posi", [128, 64], mybir.dt.int32, sco)
    trius = kk.sb("sp_trius", [128, 128], F32, sco)
    tokid = kk.sb("sp_tok", [128, 64], F32, sco)
    onescol = kk.sb("sp_ones", [128, 1], F32, sco)
    totT = kk.sb("sp_totT", [64, 1], F32, sco)
    totR = kk.sb("sp_totR", [1, 64], F32, sco)
    sent = kk.sb("sp_sent", [128, NCH], F32, sco)
    idxf = kk.sb("sp_idxf", [128, NCH], F32, sco)
    idxi = kk.sb("sp_idxi", [128, NCH], mybir.dt.int32, sco)
    idxs = kk.sb("sp_idxs", [128, NCH], mybir.dt.int32, sco)
    cmbg = kk.sb("sp_cmbg", [128, NCH], F32, sco)
    cmbm = kk.sb("sp_cmbm", [128, NCH], F32, sco)
    xs = [kk.sb(f"sp_xs{i}", [128, D], BF16, sco) for i in range(2)]
    xsT = kk.sb("sp_xsT", [128, NKC, TOKG], BF16, sco)
    glu = kk.sb("sp_glu", [128, NIC, TOKG], BF16, sco)
    w1b = [kk.sb(f"sp_w1{i}", [128, NKC, 128], BF16, sco) for i in range(2)]
    w3b = [kk.sb(f"sp_w3{i}", [128, NKC, 128], BF16, sco) for i in range(2)]
    w2b = kk.sb("sp_w2", [128, NIC, D], BF16, sco)
    s1 = [kk.sb(f"sp_s1{i}", [128, 512], F32, sco) for i in range(2)]
    yst = [kk.sb(f"sp_yst{i}", [128, D], F32, sco) for i in range(2)]

    idx_dram = [nc.dram_tensor(f"sp_idxd{i}", [NCAP + 1, 1], F32) for i in range(2)]
    cmbe_dram = [nc.dram_tensor(f"sp_cmbe{i}", [T, 1], F32) for i in range(2)]

    ps_a = kk.ps("sp_psa", [128, 512], F32, sco)
    ps_h1 = [kk.ps(f"sp_ph1{i}", [128, 512], F32, sco) for i in range(2)]
    ps_h3 = [kk.ps(f"sp_ph3{i}", [128, 512], F32, sco) for i in range(2)]
    ps_y = [kk.ps(f"sp_py{i}", [128, 512], F32, sco) for i in range(2)]
    ps_t = kk.ps("sp_pst", [128, 1024], BF16, sco)

    t_tr = R.emit("sync", lambda e: e.dma_start(out=trius[:, :], in_=io["trius"][:, :]),
                  waits=gate, inc=S["dsy"], inck=16)
    t_tk = R.emit("sync", lambda e: e.dma_start(out=tokid[:, :], in_=io["tokid"][:, :]),
                  inc=S["dsy"], inck=16)
    t_oc = R.emit("vector", lambda e: e.memset(onescol[:, :], 1.0), waits=gate, inc=S["v"])
    t_sn = R.emit("vector", lambda e: e.memset(sent[:, :], float(T)), inc=S["v"])
    zy_last = io["zy_tok"]

    cmb_toks = []
    for rblk in range(8):
        tcp = R.emit("gpsimd", lambda e, rblk=rblk: e.dma_start(
            out=cmbp[:, :],
            in_=cmbT_full[ds(H["cid"] * 2 + 16 * rblk, 2), :]),
            waits=gate + tuple(cmb_toks[-1:]), inc=S["dg"], inck=16)
        for el in range(2):
            tcs = R.emit("gpsimd", lambda e, rblk=rblk, el=el: e.dma_start(
                out=cmbe_dram[el].ap()[rblk * 1024:(rblk + 1) * 1024, :],
                in_=cmbp[el:el + 1, :]), waits=(tcp,), inc=S["cb"], inck=16)
        cmb_toks.append(tcs)
    cmb_done = (cmb_toks[-1],)

    store_toks = []
    for el in range(2):
        tsl = R.emit("sync", lambda e, el=el: e.dma_start(
            out=selsrc[:, :],
            in_=cmbe_dram[el].ap().rearrange("(p c) b -> p (c b)", c=64)),
            waits=cmb_done + tuple(store_toks[-1:]), inc=S["dsy"], inck=16)
        tse = R.emit("vector", lambda e: e.tensor_single_scalar(
            selm[:, :], selsrc[:, :], 0.0, ALU.is_gt), waits=(tsl,), inc=S["v"])
        m1 = R.emit("tensor", lambda e: e.matmul(
            ps_a[:, 0:64], trius[:, :], selm[:, :], start=True, stop=True),
            waits=(tse,), inc=S["t"])
        m2 = R.emit("tensor", lambda e: e.matmul(
            ps_a[0:1, 64:128], onescol[:, :], selm[:, :], start=True, stop=True),
            waits=(t_oc, m1), inc=S["t"])
        e1 = R.emit("vector", lambda e: e.tensor_copy(pos[:, :], ps_a[:, 0:64]),
                    waits=(m2,), inc=S["v"])
        e2 = R.emit("vector", lambda e: e.tensor_copy(totR[0:1, :], ps_a[0:1, 64:128]),
                    waits=(e1,), inc=S["v"])
        m3 = R.emit("tensor", lambda e: e.transpose(
            ps_a[0:64, 0:1], totR[0:1, :], identf[0:1, 0:1]), waits=(e2,), inc=S["t"])
        e3 = R.emit("vector", lambda e: e.tensor_copy(totT[:, :], ps_a[0:64, 0:1]),
                    waits=(m3,), inc=S["v"])
        m4 = R.emit("tensor", lambda e: e.matmul(
            ps_a[0:64, 1:2], trius[0:64, 0:64], totT[:, :], start=True, stop=True),
            waits=(e3,), inc=S["t"])
        e4 = R.emit("vector", lambda e: e.tensor_copy(totT[:, :], ps_a[0:64, 1:2]),
                    waits=(m4,), inc=S["v"])
        m5 = R.emit("tensor", lambda e: e.transpose(
            ps_a[0:1, 2:66], totT[:, :], identf[0:64, 0:64]), waits=(e4,), inc=S["t"])
        e5 = R.emit("vector", lambda e: e.tensor_copy(totR[0:1, :], ps_a[0:1, 2:66]),
                    waits=(m5,), inc=S["v"])
        m6 = R.emit("tensor", lambda e: e.matmul(
            ps_a[:, 64:128], onesf[0:1, :], totR[0:1, :], start=True, stop=True),
            waits=(e5,), inc=S["t"])
        e6 = R.emit("vector", lambda e: e.tensor_copy(post[:, :], ps_a[:, 64:128]),
                    waits=(m6,), inc=S["v"])
        tk = R.emit("vector", lambda e: e.tensor_add(pos[:, :], pos[:, :], post[:, :]),
                    waits=(e6,), inc=S["v"])
        tk = R.emit("vector", lambda e: e.tensor_mul(pos[:, :], pos[:, :], selm[:, :]),
                    waits=(tk,), inc=S["v"])
        tk = R.emit("vector", lambda e: e.scalar_tensor_tensor(
            post[:, :], selm[:, :], -float(NCAP), pos[:, :], ALU.mult, ALU.add),
            waits=(tk,), inc=S["v"])
        tk = R.emit("vector", lambda e: e.tensor_scalar_add(
            pos[:, :], post[:, :], float(NCAP)), waits=(tk,), inc=S["v"])
        tpi = R.emit("vector", lambda e: e.tensor_copy(posi[:, :], pos[:, :]),
                     waits=(tk,), inc=S["v"])
        ystf = [(), ()]
        yps = [(), ()]
        tpre = R.emit("sync", lambda e, el=el: e.dma_start(
            out=idx_dram[el].ap()[0:NCAP, :], in_=sent[:, :]),
            waits=(t_sn,) + tuple(store_toks[-1:]), inc=S["dsy"], inck=16)
        sc_toks = []
        for c in range(64):
            tsc = R.emit("gpsimd", lambda e, c=c, el=el: e.indirect_dma_start(
                out=idx_dram[el].ap()[0:1, :],
                out_offset=bass.IndirectOffsetOnAxis(posi[:, c:c + 1], 0),
                in_=tokid[:, c:c + 1], in_offset=None),
                waits=(tpi, tpre, t_tk), inc=S["ix"], inck=16)
            sc_toks.append(tsc)
        idx_done = (sc_toks[-1],)

        tixf = R.emit("sync", lambda e, el=el: e.dma_start(
            out=idxf[:, :],
            in_=idx_dram[el].ap()[0:NCAP, :].rearrange("(p j) b -> p (j b)", j=NCH)),
            waits=idx_done, inc=S["dsy"], inck=16)
        tpm = R.emit("vector", lambda e: e.tensor_single_scalar(
            cmbm[:, :], idxf[:, :], float(T) - 0.5, ALU.is_lt),
            waits=(tixf,), inc=S["v"])
        tcl = R.emit("vector", lambda e: e.tensor_scalar_min(
            pos[:, 0:NCH], idxf[:, :], float(T - 1)), waits=(tpm,), inc=S["v"])
        tixi = R.emit("vector", lambda e: e.tensor_copy(idxi[:, :], pos[:, 0:NCH]),
                      waits=(tcl,), inc=S["v"])
        tixs = R.emit("vector", lambda e: e.tensor_copy(idxs[:, :], idxf[:, :]),
                      waits=(tixi,), inc=S["v"])
        cg_toks = []
        for j in range(NCH):
            tcg = R.emit("gpsimd", lambda e, j=j, el=el: e.indirect_dma_start(
                out=cmbg[:, j:j + 1], out_offset=None,
                in_=cmbe_dram[el][0:1, :],
                in_offset=bass.IndirectOffsetOnAxis(idxi[:, j:j + 1], 0)),
                waits=(tixi,), inc=S["dg"], inck=16)
            cg_toks.append(tcg)
        tcmb = R.emit("vector", lambda e: e.tensor_mul(
            cmbg[:, :], cmbg[:, :], cmbm[:, :]),
            waits=(tpm,) + tuple(cg_toks[-1:]), inc=S["v"])

        for sbg in range(NSB):
            gat_toks = []
            xsf = [(), ()]
            tleft = ()
            for j in range(CHG):
                jj = sbg * CHG + j
                par = j % 2
                tg = R.emit("gpsimd", lambda e, jj=jj, par=par, el=el: e.indirect_dma_start(
                    out=xs[par][:, :], out_offset=None,
                    in_=xn_full[0:1, :],
                    in_offset=bass.IndirectOffsetOnAxis(idxi[:, jj:jj + 1], 0)),
                    waits=(tixi,) + xsf[par], inc=S["dg"], inck=16)
                tpv = None
                for half in range(2):
                    for kc8 in range(8):
                        kc = half * 8 + kc8
                        tpt = R.emit("tensor", lambda e, kc=kc, kc8=kc8, par=par:
                                     e.transpose(
                                         ps_t[:, kc8 * 128:(kc8 + 1) * 128],
                                         xs[par][:, kc * 128:(kc + 1) * 128],
                                         identb[:, :]),
                                     waits=(tg,) + tleft,
                                     inc=S["t"] if kc8 == 7 else None)
                    tpv = R.emit("vector", lambda e, j=j, half=half: e.tensor_copy(
                        xsT[:, half * 8:(half + 1) * 8, j * 128:(j + 1) * 128],
                        ps_t[:, :].rearrange("r (k x) -> r k x", x=128)),
                        waits=(tpt,), inc=S["v"])
                    tleft = (tpv,)
                xsf[par] = (tpv,)
                gat_toks.append(tpv)
            gat_done = (gat_toks[-1],)

            w1f = [(), ()]
            w3f = [(), ()]
            h1f = [(), ()]
            h3f = [(), ()]
            s1f = [(), ()]
            glu_toks = []
            for ic in range(NIC):
                ipar = ic % 2
                tw1 = R.emit("gpsimd", lambda e, el=el, ic=ic, ipar=ipar: e.dma_start(
                    out=w1b[ipar][:, :, :],
                    in_=_rearr_kr(w1t.ap()[el, :, ic * 128:(ic + 1) * 128])),
                    waits=w1f[ipar], inc=S["dg"], inck=16)
                tw3 = R.emit("gpsimd", lambda e, el=el, ic=ic, ipar=ipar: e.dma_start(
                    out=w3b[ipar][:, :, :],
                    in_=_rearr_kr(w3t.ap()[el, :, ic * 128:(ic + 1) * 128])),
                    inc=S["dg"], inck=16)
                for w in range(NW):
                    m1_ = m3_ = None
                    for kc in range(NKC):
                        m1_ = R.emit("tensor", lambda e, kc=kc, w=w, ipar=ipar: e.matmul(
                            ps_h1[ipar][:, :], w1b[ipar][:, kc, :],
                            xsT[:, kc, w * 512:(w + 1) * 512],
                            start=(kc == 0), stop=(kc == NKC - 1)),
                            waits=(tw1,) + gat_done + h1f[ipar],
                            inc=S["t"] if kc == NKC - 1 else None)
                    for kc in range(NKC):
                        m3_ = R.emit("tensor", lambda e, kc=kc, w=w, ipar=ipar: e.matmul(
                            ps_h3[ipar][:, :], w3b[ipar][:, kc, :],
                            xsT[:, kc, w * 512:(w + 1) * 512],
                            start=(kc == 0), stop=(kc == NKC - 1)),
                            waits=(tw3,) + h3f[ipar],
                            inc=S["t"] if kc == NKC - 1 else None)
                    if w == NW - 1:
                        w1f[ipar] = (m1_,)
                        w3f[ipar] = (m3_,)
                    spp = w % 2
                    ts1 = R.emit("scalar", lambda e, ipar=ipar, spp=spp: e.activation(
                        s1[spp][:, :], ps_h1[ipar][:, :], AF.Silu),
                        waits=(m1_,) + s1f[spp], inc=S["s"])
                    h1f[ipar] = (ts1,)
                    tg1 = R.emit("vector", lambda e, ic=ic, w=w, ipar=ipar, spp=spp:
                                 e.tensor_tensor(
                                     glu[:, ic, w * 512:(w + 1) * 512],
                                     ps_h3[ipar][:, :], s1[spp][:, :], ALU.mult),
                                 waits=(ts1, m3_), inc=S["v"])
                    h3f[ipar] = (tg1,)
                    s1f[spp] = (tg1,)
                    glu_toks.append(tg1)
            glu_done = (glu_toks[-1],)

            tw2 = R.emit("gpsimd", lambda e, el=el: e.dma_start(
                out=w2b[:, :, :], in_=_rearr_kr(w2t.ap()[el, :, :])),
                waits=glu_done, inc=S["dg"], inck=16)
            for j in range(CHG):
                jj = sbg * CHG + j
                ypar = j % 2
                ev_toks = []
                for dwq in range(4):
                    mmy = None
                    for ic in range(NIC):
                        mmy = R.emit("tensor", lambda e, ic=ic, j=j, dwq=dwq, ypar=ypar:
                                     e.matmul(
                                         ps_y[ypar][:, :],
                                         glu[:, ic, j * 128:(j + 1) * 128],
                                         w2b[:, ic, dwq * 512:(dwq + 1) * 512],
                                         start=(ic == 0), stop=(ic == NIC - 1)),
                                     waits=(tw2,) + glu_done + yps[ypar],
                                     inc=S["t"] if ic == NIC - 1 else None)
                    tye = R.emit("scalar", lambda e, jj=jj, dwq=dwq, ypar=ypar:
                                 e.activation(
                                     yst[ypar][:, dwq * 512:(dwq + 1) * 512],
                                     ps_y[ypar][:, :], AF.Copy,
                                     scale=cmbg[:, jj:jj + 1]),
                                 waits=(mmy, tcmb) + ystf[ypar], inc=S["s"])
                    yps[ypar] = (tye,)
                    ev_toks.append(tye)
                tsc = R.emit("gpsimd", lambda e, jj=jj, ypar=ypar: e.indirect_dma_start(
                    out=y_dst[0:1, :],
                    out_offset=bass.IndirectOffsetOnAxis(idxs[:, jj:jj + 1], 0),
                    in_=yst[ypar][:, :], in_offset=None, compute_op=ALU.add),
                    waits=tuple(ev_toks) + (zy_last, tixs), inc=S["dg"], inck=16)
                ystf[ypar] = (tsc,)
                store_toks.append(tsc)
    return {"stores": tuple(store_toks)}



def phase5a(kk, R, S, io, sb, H, delta, sco):
    nc = kk.nc
    xnT_full = io["xnT_full"]
    ws1t, ws3t, ws2t = io["ws1t"], io["ws3t"], io["ws2t"]

    xnT_my = kk.sb("s_xnT", [128, NKC, TSH], BF16, sco)
    glu_sh = kk.sb("s_glu", [128, NSC, TSH], BF16, sco)
    ws1b = [kk.sb(f"s_w1{i}", [128, NKC, 128], BF16, sco) for i in range(2)]
    ws3b = [kk.sb(f"s_w3{i}", [128, NKC, 128], BF16, sco) for i in range(2)]
    ws2b = [kk.sb(f"s_w2{i}", [128, 4, D], BF16, sco) for i in range(2)]
    s1 = [kk.sb(f"s_s1{i}", [128, 512], F32, sco) for i in range(2)]

    ps_u1 = [kk.ps(f"pu1{i}", [128, 512], F32, sco) for i in range(2)]
    ps_u3 = [kk.ps(f"pu3{i}", [128, 512], F32, sco) for i in range(2)]
    ps_z = [kk.ps(f"pz{i}", [128, 512], F32, sco) for i in range(2)]

    txn = R.emit("gpsimd", lambda e: e.dma_start(
        out=xnT_my[:, :, :],
        in_=xnT_full[ds(H["cid"] * 128, 128), :]
        .rearrange("r (k t) -> r k t", k=NKC)),
        waits=io.get("ag_gate", ()), inc=S["dg"], inck=16)

    w1_free = [(), ()]
    u1_free = [(), ()]
    u3_free = [(), ()]
    s1_free = [(), ()]
    glu_toks = []
    for sc in range(NSC):
        spar = sc % 2
        t1_ = R.emit("gpsimd", lambda e, sc=sc, spar=spar: e.dma_start(
            out=ws1b[spar][:, :, :],
            in_=_rearr_kr(ws1t.ap()[:, sc * 128:(sc + 1) * 128])),
            waits=w1_free[spar], inc=S["dg"], inck=16)
        t3_ = R.emit("gpsimd", lambda e, sc=sc, spar=spar: e.dma_start(
            out=ws3b[spar][:, :, :],
            in_=_rearr_kr(ws3t.ap()[:, sc * 128:(sc + 1) * 128])),
            inc=S["dg"], inck=16)
        lastg = None
        for w in range(2):
            m1_ = m3_ = None
            for kc in range(NKC):
                m1_ = R.emit("tensor", lambda e, kc=kc, w=w, spar=spar: e.matmul(
                    ps_u1[spar][:, :], ws1b[spar][:, kc, :],
                    xnT_my[:, kc, w * 512:(w + 1) * 512],
                    start=(kc == 0), stop=(kc == NKC - 1)),
                    waits=(t1_, txn) + u1_free[spar],
                    inc=S["t"] if kc == NKC - 1 else None)
            for kc in range(NKC):
                m3_ = R.emit("tensor", lambda e, kc=kc, w=w, spar=spar: e.matmul(
                    ps_u3[spar][:, :], ws3b[spar][:, kc, :],
                    xnT_my[:, kc, w * 512:(w + 1) * 512],
                    start=(kc == 0), stop=(kc == NKC - 1)),
                    waits=(t3_,) + u3_free[spar],
                    inc=S["t"] if kc == NKC - 1 else None)
            ts1 = R.emit("scalar", lambda e, spar=spar, w=w: e.activation(
                s1[w][:, :], ps_u1[spar][:, :], AF.Silu),
                waits=(m1_,) + s1_free[w], inc=S["s"])
            u1_free[spar] = (ts1,)
            tg = R.emit("vector", lambda e, sc=sc, w=w, spar=spar: e.tensor_tensor(
                glu_sh[:, sc, w * 512:(w + 1) * 512], ps_u3[spar][:, :], s1[w][:, :],
                ALU.mult), waits=(ts1, m3_), inc=S["v"])
            s1_free[w] = (tg,)
            u3_free[spar] = (tg,)
            lastg = tg
        w1_free[spar] = (lastg,)
        glu_toks.append(lastg)
    glu_done = tuple(glu_toks[-2:])
    if "dbg_glu" in io:
        R.emit("sync", lambda e: e.dma_start(
            out=io["dbg_glu"], in_=glu_sh[:, :, :].rearrange("r s t -> r (s t)")),
            waits=glu_done, inc=S["dsy"], inck=16)

    groups = [(0, 4), (4, 4), (8, 4), (12, 4), (16, 3), (19, 3)]
    w2_free = [(), ()]
    z_free = [(), ()]
    d_toks = {}
    for gi, (base, cnt) in enumerate(groups):
        gpar = gi % 2
        tw2 = R.emit("gpsimd", lambda e, base=base, cnt=cnt, gpar=gpar: e.dma_start(
            out=ws2b[gpar][:, 0:cnt, :],
            in_=_rearr_kr(ws2t.ap()[base * 128:(base + cnt) * 128, :])),
            waits=w2_free[gpar], inc=S["dg"], inck=16)
        for tt in range(8):
            for dw in range(4):
                zpar = (tt * 4 + dw) % 2
                mmz = None
                for i2 in range(cnt):
                    mmz = R.emit("tensor", lambda e, base=base, i2=i2, tt=tt, dw=dw,
                                 gpar=gpar, zpar=zpar, first=(i2 == 0),
                                 last=(i2 == cnt - 1): e.matmul(
                                     ps_z[zpar][:, :],
                                     glu_sh[:, base + i2, tt * 128:(tt + 1) * 128],
                                     ws2b[gpar][:, i2, dw * 512:(dw + 1) * 512],
                                     start=first, stop=last),
                                 waits=(tw2,) + glu_done + z_free[zpar],
                                 inc=S["t"] if i2 == cnt - 1 else None)
                if gi == 0:
                    td = R.emit("vector", lambda e, tt=tt, dw=dw, zpar=zpar: e.tensor_copy(
                        delta[:, tt, dw * 512:(dw + 1) * 512], ps_z[zpar][:, :]),
                        waits=(mmz,), inc=S["v"])
                else:
                    td = R.emit("vector", lambda e, tt=tt, dw=dw, zpar=zpar: e.tensor_tensor(
                        delta[:, tt, dw * 512:(dw + 1) * 512],
                        delta[:, tt, dw * 512:(dw + 1) * 512], ps_z[zpar][:, :], ALU.add),
                        waits=(mmz,) + d_toks.get((tt, dw), ()), inc=S["v"])
                d_toks[(tt, dw)] = (td,)
                z_free[zpar] = (td,)
        w2_free[gpar] = (mmz,)
    return d_toks


def phase5b(kk, R, S, io, sb, H, delta, d_toks, sco):
    nc = kk.nc
    aT_full = io["aT_full"]
    wo, bo = io["wo"], io["bo"]
    onesf = sb["onesf"]

    aT_my = kk.sb("o_aT", [128, NH, TSH], BF16, sco)
    wo_sb = kk.sb("o_wo", [128, NH, D], BF16, sco)
    bo_r = kk.sb("o_bo", [1, D], F32, sco)
    ps_o = [kk.ps(f"po{i}", [128, 512], F32, sco) for i in range(2)]

    ta_toks = []
    for h in range(NH):
        eng = "sync" if h < 8 else "scalar"
        mybk = "myb_s" if h < 8 else "myb_sc"
        qofk = "qoff_s" if h < 8 else "qoff_sc"
        ta = R.emit(eng, lambda e, h=h, mybk=mybk, qofk=qofk: e.dma_start(
            out=aT_my[:, h, :],
            in_=aT_full[ds(512 * (h // 2) + 256 * (h % 2) + H[mybk] * 128, 128),
                        ds(H[qofk], TSH)]),
            waits=io.get("ag_gate", ()),
            inc=S["dsy"] if h < 8 else S["dsc"], inck=16)
        ta_toks.append(ta)
    ta_toks = tuple(ta_toks)
    two = R.emit("gpsimd", lambda e: e.dma_start(
        out=wo_sb[:, :, :], in_=_rearr_kr(wo.ap())), inc=S["dg"], inck=16)
    tbo = R.emit("sync", lambda e: e.dma_start(out=bo_r[:, :], in_=bo[:, :]),
                 inc=S["dsy"], inck=16)
    o_free = [(), ()]
    for tt in range(8):
        for dw in range(4):
            opar = (tt * 4 + dw) % 2
            for h in range(NH):
                R.emit("tensor", lambda e, h=h, tt=tt, dw=dw, opar=opar: e.matmul(
                    ps_o[opar][:, :], aT_my[:, h, tt * 128:(tt + 1) * 128],
                    wo_sb[:, h, dw * 512:(dw + 1) * 512],
                    start=(h == 0), stop=False),
                    waits=ta_toks + (two,) + o_free[opar])
            mmo = R.emit("tensor", lambda e, dw=dw, opar=opar: e.matmul(
                ps_o[opar][:, :], onesf[0:1, :], bo_r[0:1, dw * 512:(dw + 1) * 512],
                start=False, stop=True), waits=(tbo,), inc=S["t"])
            td = R.emit("vector", lambda e, tt=tt, dw=dw, opar=opar: e.tensor_tensor(
                delta[:, tt, dw * 512:(dw + 1) * 512],
                delta[:, tt, dw * 512:(dw + 1) * 512], ps_o[opar][:, :], ALU.add),
                waits=(mmo,) + d_toks.get((tt, dw), ()), inc=S["v"])
            d_toks[(tt, dw)] = (td,)
            o_free[opar] = (td,)
    return d_toks


def phase6(kk, R, S, io, sb, H, delta, d_toks, sco):
    x_sh, out_sh = io["x_sh"], io["out_sh"]
    ym = [kk.sb(f"f_ym{i}", [128, D], F32 if io.get("ym_f32") else BF16, sco)
          for i in range(2)]
    xt = [kk.sb(f"f_xt{i}", [128, D], F32, sco) for i in range(2)]
    ost = [kk.sb(f"f_os{i}", [128, D], F32, sco) for i in range(2)]
    store_toks = []
    ym_free = [(), ()]
    xt_free = [(), ()]
    ost_free = [(), ()]
    ysrc = io.get("y_sl")
    ymy = io.get("y_my")
    for tt in range(8):
        dts = d_toks.get((tt, 0), ()) + d_toks.get((tt, 1), ()) + \
            d_toks.get((tt, 2), ()) + d_toks.get((tt, 3), ())
        nsrc = 8 if ysrc is not None else 1
        for r in range(nsrc):
            par = r % 2
            if ysrc is not None:
                tld = R.emit("sync", lambda e, r=r, tt=tt, par=par: e.dma_start(
                    out=ym[par][:, :], in_=ysrc[r, tt * 128:(tt + 1) * 128, :]),
                    waits=ym_free[par], inc=S["dsy"], inck=16)
            else:
                tld = R.emit("sync", lambda e, tt=tt, par=par: e.dma_start(
                    out=ym[par][:, :], in_=ymy[tt * 128:(tt + 1) * 128, :]),
                    waits=ym_free[par] + io.get("rs_done", ()), inc=S["dsy"], inck=16)
            ty = R.emit("vector", lambda e, tt=tt, par=par: e.tensor_tensor(
                delta[:, tt, :], delta[:, tt, :], ym[par][:, :], ALU.add),
                waits=(tld,) + dts, inc=S["v"])
            ym_free[par] = (ty,)
            dts = (ty,)
        par = tt % 2
        tldx = R.emit("sync", lambda e, tt=tt, par=par: e.dma_start(
            out=xt[par][:, :], in_=x_sh[tt * 128:(tt + 1) * 128, :]),
            waits=xt_free[par], inc=S["dsy"], inck=16)
        tout = R.emit("vector", lambda e, tt=tt, par=par: e.tensor_tensor(
            ost[par][:, :], delta[:, tt, :], xt[par][:, :], ALU.add),
            waits=(tldx,) + dts + ost_free[par], inc=S["v"])
        xt_free[par] = (tout,)
        tst = R.emit("sync", lambda e, tt=tt, par=par: e.dma_start(
            out=out_sh[tt * 128:(tt + 1) * 128, :], in_=ost[par][:, :]),
            waits=(tout,), inc=S["dsy"], inck=16)
        ost_free[par] = (tst,)
        store_toks.append(tst)
    return {"stores": tuple(store_toks)}


def _phase_gate(R, toks):
    for eng in ("sync", "gpsimd", "scalar", "vector", "tensor"):
        R.emit(eng, lambda e: e.nop(), waits=toks)


def _emit_getcid(R, H):
    def getcid(e):
        H["cid"] = e.partition_id()
        H["myb"] = H["cid"] // 4
        H["qoff"] = (H["cid"] % 4) * TSH
        return e.nop()
    R.emit("gpsimd", getcid)

    def getcid_s(e):
        H["cid_s"] = e.partition_id()
        H["myb_s"] = H["cid_s"] // 4
        H["qoff_s"] = (H["cid_s"] % 4) * TSH
        return e.nop()
    R.emit("sync", getcid_s)

    def getcid_sc(e):
        H["cid_sc"] = e.partition_id()
        H["myb_sc"] = H["cid_sc"] // 4
        H["qoff_sc"] = (H["cid_sc"] % 4) * TSH
        return e.nop()
    R.emit("scalar", getcid_sc)



def _mk_common(kk, R, S, nc, io):
    sb = {}
    sb["identb"] = kk.sb("identb", [128, 128], BF16)
    sb["identf"] = kk.sb("identf", [128, 128], F32)
    sb["onesf"] = kk.sb("onesf", [1, 128], F32)
    sb["epscol"] = kk.sb("epscol", [128, 1], F32)
    R.emit("gpsimd", lambda e: e.dma_start(out=sb["identb"][:, :], in_=io["identity"][:, :]),
           inc=S["dg"], inck=16)
    R.emit("sync", lambda e: e.dma_start(out=sb["identf"][:, :], in_=io["identity"][:, :]),
           inc=S["dsy"], inck=16)
    R.emit("vector", lambda e: e.memset(sb["onesf"][:, :], 1.0), inc=S["v"])
    R.emit("vector", lambda e: e.memset(sb["epscol"][:, :], EPS), inc=S["v"])
    return sb


def _mk_sems(kk):
    return {n: kk.sem(n) for n in ("dsy", "dg", "dsc", "t", "v", "s", "cc",
                                   "zy", "ix", "cb")}


def _emit_final(R, S, stores):
    def fin(e, stores=stores):
        return e.nop()
    R.emit("sync", fin, waits=stores)



def _dram_in(nc, name, shape, dt):
    return nc.dram_tensor(name, list(shape), dt, kind="ExternalInput")


def _dram_out(nc, name, shape, dt):
    return nc.dram_tensor(name, list(shape), dt, kind="ExternalOutput")


def build_L1():
    nc = bass.Bass("TRN2", target_bir_lowering=False)
    kk = K(nc)
    io = {
        "x_sh": _dram_in(nc, "x_sh", [TSH, D], F32).ap(),
        "gwT": _dram_in(nc, "gwT", [D, 16], F32),
        "identity": _dram_in(nc, "identity", [128, 128], F32).ap(),
        "xnT_dst": _dram_out(nc, "xnT_sh", [128, NKC * TSH], BF16).ap(),
        "sm_dst": _dram_out(nc, "sm_sh", [TSH, 4], F32).ap(),
        "cmbT_dst": _dram_out(nc, "cmbT_sh", [16, TSH], F32).ap(),
    }
    R = Rec()
    S = _mk_sems(kk)
    with nc.Block() as block:
        sb = _mk_common(kk, R, S, nc, io)
        with contextlib.ExitStack() as sco:
            res = phase1(kk, R, S, io, sb, sco)
            _emit_final(R, S, res["stores"])
            R.flush(block)
    kk.stack.close()
    return nc


def build_L2():
    nc = bass.Bass("TRN2", target_bir_lowering=False)
    kk = K(nc)
    io = {
        "xnT_full": _dram_in(nc, "xnT_full", [NC * 128, NKC * TSH], BF16).ap(),
        "sm_full": _dram_in(nc, "sm_full", [T, 4], F32).ap(),
        "wq": _dram_in(nc, "wq", [D, 256], F32),
        "wk": _dram_in(nc, "wk", [D, 256], F32),
        "wv": _dram_in(nc, "wv", [D, 256], F32),
        "bq": _dram_in(nc, "bq", [1, 256], F32).ap(),
        "bk": _dram_in(nc, "bk", [1, 256], F32).ap(),
        "bv": _dram_in(nc, "bv", [1, 256], F32).ap(),
        "qn_g": _dram_in(nc, "qn_g", [D], F32),
        "qn_b": _dram_in(nc, "qn_b", [D], F32),
        "kvn_g": _dram_in(nc, "kvn_g", [D], F32),
        "kvn_b": _dram_in(nc, "kvn_b", [D], F32),
        "ln_g": _dram_in(nc, "ln_g", [1, 128], F32).ap(),
        "ln_b": _dram_in(nc, "ln_b", [1, 128], F32).ap(),
        "cos2": _dram_in(nc, "cos2", [C, 128], F32),
        "sin2": _dram_in(nc, "sin2", [C, 128], F32),
        "identity": _dram_in(nc, "identity", [128, 128], F32).ap(),
        "diagmask": _dram_in(nc, "diagmask", [128, 128], F32).ap(),
        "trius": _dram_in(nc, "trius", [128, 128], F32).ap(),
        "tokid": _dram_in(nc, "tokid", [128, 64], F32).ap(),
        "aT_dst": _dram_out(nc, "aT_sh", [512, C], BF16).ap(),
    }
    R = Rec()
    S = _mk_sems(kk)
    with nc.Block() as block:
        sb = _mk_common(kk, R, S, nc, io)
        with contextlib.ExitStack() as sco2:
            p2 = phase23(kk, R, S, io, sb, sco2)
            R.flush(block)
    with nc.Block() as block:
        with contextlib.ExitStack() as sco3:
            res = phase_attn(kk, R, S, io, sb, p2, sco3)
            _emit_final(R, S, res["stores"])
            R.flush(block)
    kk.stack.close()
    return nc


def build_L3():
    nc = bass.Bass("TRN2", target_bir_lowering=False)
    kk = K(nc)
    io = {
        "xnT_full": _dram_in(nc, "xnT_full", [NC * 128, NKC * TSH], BF16).ap(),
        "cmbT_full": _dram_in(nc, "cmbT_full", [NC * 16, TSH], F32).ap(),
        "identity": _dram_in(nc, "identity", [128, 128], F32).ap(),
        "w1t": _dram_in(nc, "w1t", [2, D, EI], F32),
        "w3t": _dram_in(nc, "w3t", [2, D, EI], F32),
        "w2t": _dram_in(nc, "w2t", [2, EI, D], F32),
        "y_dst": _dram_out(nc, "y_sh", [T, D], BF16).ap(),
    }
    R = Rec()
    S = _mk_sems(kk)
    kk_sb = {"identb": kk.sb("identb", [128, 128], BF16)}
    H = {}
    with nc.Block() as block:
        R.emit("gpsimd", lambda e: e.dma_start(out=kk_sb["identb"][:, :],
                                               in_=io["identity"][:, :]),
               inc=S["dg"], inck=16)

        _emit_getcid(R, H)
        with contextlib.ExitStack() as sco:
            res = phase4(kk, R, S, io, kk_sb, H, sco)
            _emit_final(R, S, res["stores"])
            R.flush(block)
    kk.stack.close()
    return nc


def build_L4():
    nc = bass.Bass("TRN2", target_bir_lowering=False)
    kk = K(nc)
    io = {
        "xnT_full": _dram_in(nc, "xnT_full", [NC * 128, NKC * TSH], BF16).ap(),
        "aT_full": _dram_in(nc, "aT_full", [NC * 512, C], BF16).ap(),
        "ws1t": _dram_in(nc, "ws1t", [D, SHD], F32),
        "ws3t": _dram_in(nc, "ws3t", [D, SHD], F32),
        "ws2t": _dram_in(nc, "ws2t", [SHD, D], F32),
        "wo": _dram_in(nc, "wo", [D, D], F32),
        "bo": _dram_in(nc, "bo", [1, D], F32).ap(),
        "x_sh": _dram_in(nc, "x_sh", [TSH, D], F32).ap(),
        "y_sl": _dram_in(nc, "y_sl", [NC, TSH, D], BF16).ap(),
        "out_sh": _dram_out(nc, "out_sh", [TSH, D], F32).ap(),
    }
    R = Rec()
    S = _mk_sems(kk)
    kk_sb = {"onesf": kk.sb("onesf", [1, 128], F32)}
    delta = kk.sb("d_acc", [128, 8, D], F32)
    H = {}
    with nc.Block() as block:
        R.emit("vector", lambda e: e.memset(kk_sb["onesf"][:, :], 1.0), inc=S["v"])

        _emit_getcid(R, H)
        with contextlib.ExitStack() as sco:
            d_toks = phase5a(kk, R, S, io, kk_sb, H, delta, sco)
            R.flush(block)
    with nc.Block() as block:
        with contextlib.ExitStack() as sco:
            d_toks = phase5b(kk, R, S, io, kk_sb, H, delta, d_toks, sco)
            R.flush(block)
    with nc.Block() as block:
        with contextlib.ExitStack() as sco:
            res = phase6(kk, R, S, io, kk_sb, H, delta, d_toks, sco)
            _emit_final(R, S, res["stores"])
            R.flush(block)
    kk.stack.close()
    return nc



def _host_prep(inputs):
    x = inputs["L0_attn_x"]
    f64 = x.dtype != np.float32

    def cst(a):
        a = np.asarray(a)
        return a if f64 else np.ascontiguousarray(a.astype(np.float32))

    Wq = np.asarray(inputs["L0_attn_Wq"])
    Wkv = np.asarray(inputs["L0_attn_Wkv"])
    Wo = np.asarray(inputs["L0_attn_Wo"])
    W1 = np.asarray(inputs["L0_ffn_W1"])
    W2 = np.asarray(inputs["L0_ffn_W2"])
    W3 = np.asarray(inputs["L0_ffn_W3"])
    Ws1 = np.asarray(inputs["L0_ffn_Ws1"])
    Ws2 = np.asarray(inputs["L0_ffn_Ws2"])
    Ws3 = np.asarray(inputs["L0_ffn_Ws3"])
    gate_w = np.asarray(inputs["L0_ffn_gate_w"])

    xflat = np.ascontiguousarray(np.asarray(x).reshape(T, D))

    pos = np.arange(C, dtype=np.float64)
    freqs = np.arange(0, HD, 2, dtype=np.float64) / HD * math.log(30000.0)
    ang = pos[:, None] * freqs[None, :]
    cosd = np.concatenate([np.cos(ang), np.cos(ang)], axis=1)
    sind = np.concatenate([np.sin(ang), np.sin(ang)], axis=1)

    ident = np.eye(128, dtype=np.float64)
    diag = np.triu(np.ones((128, 128), dtype=np.float64))
    trius = np.triu(np.ones((128, 128), dtype=np.float64), k=1)
    tokid = (np.arange(128)[:, None] * 64 + np.arange(64)[None, :]).astype(np.float64)

    perm = np.array(PERM)
    com = {
        "identity": cst(ident),
    }
    per_core = []
    for c in range(NC):
        qcols = np.concatenate([(2 * c + hl) * 128 + perm for hl in range(2)])
        vcols = np.concatenate([D + (2 * c + hl) * 128 + np.arange(128) for hl in range(2)])
        kcols = np.concatenate([(2 * c + hl) * 128 + perm for hl in range(2)])
        d = {
            "x_sh": np.ascontiguousarray(xflat[c * TSH:(c + 1) * TSH]),
            "gwT": np.ascontiguousarray(gate_w.T),
            "identity": com["identity"],
            "wq": np.ascontiguousarray(Wq[:, qcols]),
            "wk": np.ascontiguousarray(Wkv[:, kcols]),
            "wv": np.ascontiguousarray(Wkv[:, vcols]),
            "bq": np.ascontiguousarray(np.asarray(inputs["L0_attn_bq"])[qcols][None, :]),
            "bk": np.ascontiguousarray(np.asarray(inputs["L0_attn_bkv"])[kcols][None, :]),
            "bv": np.ascontiguousarray(np.asarray(inputs["L0_attn_bkv"])[D:][vcols - D][None, :]),
            "qn_g": np.ascontiguousarray(np.asarray(inputs["L0_attn_qn_g"])),
            "qn_b": np.ascontiguousarray(np.asarray(inputs["L0_attn_qn_b"])),
            "kvn_g": np.ascontiguousarray(np.asarray(inputs["L0_attn_kvn_g"])),
            "kvn_b": np.ascontiguousarray(np.asarray(inputs["L0_attn_kvn_b"])),
            "ln_g": np.ascontiguousarray(np.asarray(inputs["L0_attn_ln_g"])[perm][None, :]),
            "ln_b": np.ascontiguousarray(np.asarray(inputs["L0_attn_ln_b"])[perm][None, :]),
            "cos2": cst(cosd),
            "sin2": cst(sind),
            "diagmask": cst(diag),
            "trius": cst(trius),
            "tokid": cst(tokid),
            "w1t": np.ascontiguousarray(W1[2 * c:2 * c + 2].transpose(0, 2, 1)),
            "w3t": np.ascontiguousarray(W3[2 * c:2 * c + 2].transpose(0, 2, 1)),
            "w2t": np.ascontiguousarray(W2[2 * c:2 * c + 2].transpose(0, 2, 1)),
            "ws1t": np.ascontiguousarray(Ws1.T),
            "ws3t": np.ascontiguousarray(Ws3.T),
            "ws2t": np.ascontiguousarray(Ws2.T),
            "wo": np.ascontiguousarray(Wo),
            "bo": np.ascontiguousarray(np.asarray(inputs["L0_attn_bo"])[None, :]),
        }
        per_core.append(d)
    return per_core


def _sel(d, keys):
    return {k: d[k] for k in keys}


def kernel(**inputs):
    per_core = _host_prep(inputs)
    cores = list(range(NC))

    if MODE == "4launch":
        nc1 = build_L1()
        r1 = run_bass_kernel_spmd(
            nc1, [_sel(per_core[c], ("x_sh", "gwT", "identity")) for c in cores],
            core_ids=cores)
        xnT_full = np.concatenate([np.asarray(r1.results[c]["xnT_sh"]) for c in cores], 0)
        sm_full = np.concatenate([np.asarray(r1.results[c]["sm_sh"]) for c in cores], 0)
        cmbT_full = np.concatenate([np.asarray(r1.results[c]["cmbT_sh"]) for c in cores], 0)
        if _CAPTURE:
            _DBG.update(xnT=xnT_full, sm=sm_full, cmbT=cmbT_full)
        if _STOP_AFTER == 1:
            return None

        nc2 = build_L2()
        l2keys = ("wq", "wk", "wv", "bq", "bk", "bv", "qn_g", "qn_b", "kvn_g",
                  "kvn_b", "ln_g", "ln_b", "cos2", "sin2", "identity", "diagmask")
        r2 = run_bass_kernel_spmd(
            nc2, [dict(_sel(per_core[c], l2keys), xnT_full=xnT_full, sm_full=sm_full)
                  for c in cores], core_ids=cores)
        aT_full = np.concatenate([np.asarray(r2.results[c]["aT_sh"]) for c in cores], 0)
        if _CAPTURE:
            _DBG.update(aT=aT_full)
        if _STOP_AFTER == 2:
            return None

        nc3 = build_L3()
        r3 = run_bass_kernel_spmd(
            nc3, [dict(_sel(per_core[c], ("w1t", "w3t", "w2t", "identity")),
                       xnT_full=xnT_full, cmbT_full=cmbT_full) for c in cores],
            core_ids=cores)
        y_parts = [np.asarray(r3.results[c]["y_sh"]) for c in cores]
        if _CAPTURE:
            _DBG.update(y_parts=y_parts)
        if _STOP_AFTER == 3:
            return None

        nc4 = build_L4()
        l4keys = ("ws1t", "ws3t", "ws2t", "wo", "bo", "x_sh")
        in4 = []
        for c in cores:
            y_sl = np.stack([y_parts[r][c * TSH:(c + 1) * TSH] for r in cores])
            in4.append(dict(_sel(per_core[c], l4keys), xnT_full=xnT_full,
                            aT_full=aT_full, y_sl=np.ascontiguousarray(y_sl)))
        r4 = run_bass_kernel_spmd(nc4, in4, core_ids=cores)
        out = np.concatenate([np.asarray(r4.results[c]["out_sh"]) for c in cores], 0)
        return out.reshape(B, C, D)

    if MODE == "fused":
        return _kernel_fused(per_core, cores)
    raise NotImplementedError(MODE)



def build_fused():
    nc = bass.Bass("TRN2", target_bir_lowering=False)
    kk = K(nc)
    base = {
        "x_sh": _dram_in(nc, "x_sh", [TSH, D], F32).ap(),
        "gwT": _dram_in(nc, "gwT", [D, 16], F32),
        "identity": _dram_in(nc, "identity", [128, 128], F32).ap(),
        "wq": _dram_in(nc, "wq", [D, 256], F32),
        "wk": _dram_in(nc, "wk", [D, 256], F32),
        "wv": _dram_in(nc, "wv", [D, 256], F32),
        "bq": _dram_in(nc, "bq", [1, 256], F32).ap(),
        "bk": _dram_in(nc, "bk", [1, 256], F32).ap(),
        "bv": _dram_in(nc, "bv", [1, 256], F32).ap(),
        "qn_g": _dram_in(nc, "qn_g", [D], F32),
        "qn_b": _dram_in(nc, "qn_b", [D], F32),
        "kvn_g": _dram_in(nc, "kvn_g", [D], F32),
        "kvn_b": _dram_in(nc, "kvn_b", [D], F32),
        "ln_g": _dram_in(nc, "ln_g", [1, 128], F32).ap(),
        "ln_b": _dram_in(nc, "ln_b", [1, 128], F32).ap(),
        "cos2": _dram_in(nc, "cos2", [C, 128], F32),
        "sin2": _dram_in(nc, "sin2", [C, 128], F32),
        "diagmask": _dram_in(nc, "diagmask", [128, 128], F32).ap(),
        "trius": _dram_in(nc, "trius", [128, 128], F32).ap(),
        "tokid": _dram_in(nc, "tokid", [128, 64], F32).ap(),
        "w1t": _dram_in(nc, "w1t", [2, D, EI], F32),
        "w3t": _dram_in(nc, "w3t", [2, D, EI], F32),
        "w2t": _dram_in(nc, "w2t", [2, EI, D], F32),
        "ws1t": _dram_in(nc, "ws1t", [D, SHD], F32),
        "ws3t": _dram_in(nc, "ws3t", [D, SHD], F32),
        "ws2t": _dram_in(nc, "ws2t", [SHD, D], F32),
        "wo": _dram_in(nc, "wo", [D, D], F32),
        "bo": _dram_in(nc, "bo", [1, D], F32).ap(),
        "out_sh": _dram_out(nc, "out_sh", [TSH, D], F32).ap(),
    }
    ag_xnT_in = nc.dram_tensor("ag_xnT_in", [128, NKC * TSH], BF16)
    ag_xnT_out = nc.dram_tensor("ag_xnT_out", [NC * 128, NKC * TSH], BF16)
    ag_sm_in = nc.dram_tensor("ag_sm_in", [TSH, 4], F32)
    ag_sm_out = nc.dram_tensor("ag_sm_out", [T, 4], F32)
    ag_ct_in = nc.dram_tensor("ag_ct_in", [16, TSH], F32)
    ag_ct_out = nc.dram_tensor("ag_ct_out", [NC * 16, TSH], F32)
    ag_aT_in = nc.dram_tensor("ag_aT_in", [512, C], BF16)
    ag_aT_out = nc.dram_tensor("ag_aT_out", [NC * 512, C], BF16)
    YDT = F32 if SPARSE_MOE else BF16
    ag_xn_in = nc.dram_tensor("ag_xn_in", [TSH, D], BF16)
    ag_xn_out = nc.dram_tensor("ag_xn_out", [T, D], BF16)
    y_part = nc.dram_tensor("y_part", [T + 1, D] if SPARSE_MOE else [T, D], YDT)
    y_my = nc.dram_tensor("y_my", [TSH, D], YDT)

    R = Rec()
    S = _mk_sems(kk)
    H = {}
    rg = [list(range(NC))]

    with nc.Block() as block:
        _emit_getcid(R, H)
        sb = _mk_common(kk, R, S, nc, base)
        io1 = dict(base, xnT_dst=ag_xnT_in.ap(), sm_dst=ag_sm_in.ap(),
                   cmbT_dst=ag_ct_in.ap())
        if SPARSE_MOE:
            io1["xn_dst"] = ag_xn_in.ap()
        sco1 = contextlib.ExitStack()
        res1 = phase1(kk, R, S, io1, sb, sco1)
        st1 = res1["stores"]
        t1 = R.emit("gpsimd", lambda e: e.collective_compute(
            "AllGather", ALU.bypass, replica_groups=rg,
            ins=[ag_xnT_in.ap()], outs=[ag_xnT_out.ap()]),
            waits=st1, inc=S["cc"])
        t2 = R.emit("gpsimd", lambda e: e.collective_compute(
            "AllGather", ALU.bypass, replica_groups=rg,
            ins=[ag_sm_in.ap()], outs=[ag_sm_out.ap()]),
            waits=(t1,), inc=S["cc"])
        t3 = R.emit("gpsimd", lambda e: e.collective_compute(
            "AllGather", ALU.bypass, replica_groups=rg,
            ins=[ag_ct_in.ap()], outs=[ag_ct_out.ap()]),
            waits=(t2,), inc=S["cc"])
        if SPARSE_MOE:
            t3 = R.emit("gpsimd", lambda e: e.collective_compute(
                "AllGather", ALU.bypass, replica_groups=rg,
                ins=[ag_xn_in.ap()], outs=[ag_xn_out.ap()]),
                waits=(t3,), inc=S["cc"])
        R.flush(block)
        sco1.close()
    ag3 = (t3,)

    io2 = dict(base, xnT_full=ag_xnT_out.ap(), sm_full=ag_sm_out.ap(),
               aT_dst=ag_aT_in.ap(), ag_gate=ag3)
    zy_tok = None
    with contextlib.ExitStack() as sco23:
        with nc.Block() as block:
            _phase_gate(R, st1 + ag3)
            with contextlib.ExitStack() as sco2:
                p2 = phase23(kk, R, S, io2, sb, sco2, sco23)
                R.flush(block)
        with nc.Block() as block:
            _phase_gate(R, st1)
            with contextlib.ExitStack() as sco3:
                if SPARSE_MOE:
                    zb = kk.sb("zy_row", [128, D], F32, sco3)
                    tzb = R.emit("vector", lambda e: e.memset(zb[:, :], 0.0),
                                 inc=S["v"])
                    for i in range(T // 128):
                        zy_tok = R.emit("scalar", lambda e, i=i: e.dma_start(
                            out=y_part.ap()[i * 128:(i + 1) * 128, :],
                            in_=zb[:, :]), waits=(tzb,), inc=S["zy"], inck=16)
                res3 = phase_attn(kk, R, S, io2, sb, p2, sco3)
                t4 = R.emit("gpsimd", lambda e: e.collective_compute(
                    "AllGather", ALU.bypass, replica_groups=rg,
                    ins=[ag_aT_in.ap()], outs=[ag_aT_out.ap()]),
                    waits=res3["stores"] + ag3, inc=S["cc"])
                R.flush(block)
    ag4 = (t4,)

    io4 = dict(base, xnT_full=ag_xnT_out.ap(), cmbT_full=ag_ct_out.ap(),
               xn_full=ag_xn_out.ap(), y_dst=y_part.ap(), ag_gate=ag4)
    if SPARSE_MOE:
        io4["zy_tok"] = zy_tok
    with nc.Block() as block:
        _phase_gate(R, res3["stores"][-1:] + ag4
                    + ((zy_tok,) if SPARSE_MOE else ()))
        with contextlib.ExitStack() as sco4:
            if SPARSE_MOE:
                res4 = phase4_sparse(kk, R, S, io4, sb, H, sco4)
            else:
                res4 = phase4(kk, R, S, io4, sb, H, sco4)
            t5 = R.emit("gpsimd", lambda e: e.collective_compute(
                "ReduceScatter", ALU.add, replica_groups=rg,
                ins=[y_part.ap()[0:T, :] if SPARSE_MOE else y_part.ap()],
                outs=[y_my.ap()]),
                waits=res4["stores"] + ag4, inc=S["cc"])
            R.flush(block)
    rs5 = (t5,)

    delta = kk.sb("d_acc", [128, 8, D], F32)
    io5 = dict(base, xnT_full=ag_xnT_out.ap(), aT_full=ag_aT_out.ap(),
               ag_gate=rs5)
    if _DEBUG_IO:
        io5["dbg_glu"] = nc.dram_tensor(
            "dbg_glu", [128, NSC * TSH], BF16, kind="ExternalOutput").ap()
    with nc.Block() as block:
        _phase_gate(R, res4["stores"][-1:] + rs5)
        with contextlib.ExitStack() as sco:
            d_toks = phase5a(kk, R, S, io5, sb, H, delta, sco)
            R.flush(block)
    if _DEBUG_IO:
        d5a = nc.dram_tensor("dbg_d5a", [128, 8 * D], F32, kind="ExternalOutput")
        with nc.Block() as block:
            last = [d_toks[k][-1] for k in d_toks]
            R.emit("sync", lambda e: e.dma_start(
                out=d5a.ap(), in_=delta[:, :, :].rearrange("r t d -> r (t d)")),
                waits=tuple(last), inc=S["dsy"], inck=16)
            R.flush(block)
    io5b = dict(io5, ag_gate=ag4)
    with nc.Block() as block:
        _phase_gate(R, res4["stores"][-1:])
        with contextlib.ExitStack() as sco:
            d_toks = phase5b(kk, R, S, io5b, sb, H, delta, d_toks, sco)
            R.flush(block)
    if _DEBUG_IO:
        d5b = nc.dram_tensor("dbg_d5b", [128, 8 * D], F32, kind="ExternalOutput")
        with nc.Block() as block:
            last = [d_toks[k][-1] for k in d_toks]
            R.emit("sync", lambda e: e.dma_start(
                out=d5b.ap(), in_=delta[:, :, :].rearrange("r t d -> r (t d)")),
                waits=tuple(last), inc=S["dsy"], inck=16)
            R.flush(block)
    io6 = dict(base, y_my=y_my.ap(), rs_done=rs5, ym_f32=SPARSE_MOE)
    with nc.Block() as block:
        _phase_gate(R, res4["stores"][-1:])
        with contextlib.ExitStack() as sco:
            res6 = phase6(kk, R, S, io6, sb, H, delta, d_toks, sco)
            _emit_final(R, S, res6["stores"])
            R.flush(block)
    if _DEBUG_IO:
        dbg = {
            "dbg_xnT": (ag_xnT_in, [128, NKC * TSH], BF16),
            "dbg_sm": (ag_sm_in, [TSH, 4], F32),
            "dbg_ct": (ag_ct_in, [16, TSH], F32),
            "dbg_aT": (ag_aT_in, [512, C], BF16),
            "dbg_ymy": (y_my, [TSH, D], BF16),
            "dbg_ypart": (y_part, [T, D], BF16),
        }
        with nc.Block() as block:
            def emitdbg(e):
                for nm, (src, shp, dt_) in dbg.items():
                    o = nc.dram_tensor(nm, shp, dt_, kind="ExternalOutput")
                    sap = src.ap()
                    if list(sap.tensor.shape)[0] > shp[0]:
                        sap = sap[0:shp[0], :]
                    e.dma_start(out=o.ap(), in_=sap).then_inc(S["dg"].h, 16)
                    S["dg"].n += 16
                e.wait_ge(S["dg"].h, S["dg"].n)
                return e.nop()
            R.emit("gpsimd", emitdbg, waits=res6["stores"])
            R.flush(block)
    kk.stack.close()
    return nc


def _kernel_fused(per_core, cores):
    nc = build_fused()
    keys = ("x_sh", "gwT", "identity", "wq", "wk", "wv", "bq", "bk", "bv",
            "qn_g", "qn_b", "kvn_g", "kvn_b", "ln_g", "ln_b", "cos2", "sin2",
            "diagmask", "trius", "tokid", "w1t", "w3t", "w2t", "ws1t", "ws3t",
            "ws2t", "wo", "bo")
    r = run_bass_kernel_spmd(nc, [_sel(per_core[c], keys) for c in cores],
                             core_ids=cores)
    if _DEBUG_IO and _CAPTURE:
        _DBG["fused"] = {c: {k: np.asarray(v) for k, v in r.results[c].items()}
                         for c in cores}
    out = np.concatenate([np.asarray(r.results[c]["out_sh"]) for c in cores], 0)
    return out.reshape(B, C, D)
